# version 26
# baseline (speedup 1.0000x reference)
import numpy as np

import concourse.bass as bass
import concourse.mybir as mybir
from concourse.bass_utils import run_bass_kernel_spmd

F32 = mybir.dt.float32
BF16 = mybir.dt.bfloat16
AF = mybir.ActivationFunctionType
ALU = mybir.AluOpType
AX = mybir.AxisListType

RC1, RC2, RC3 = 6291456.0, 12582912.0, 25165824.0

T = 4096
D = 1024
I2 = 2816
IH = 1408
CH = 512
NCH = T // CH
NTL = T // 128
LN_EPS = 1e-5
AT_EPS = 1e-6

PHASES = ("a0", "a1", "a2", "b1", "b2")
PARANOID = True
DEBUG_OUT = False

ENGINES = ("pe", "dve", "act", "sp", "gp")


class Planner:

    def __init__(self):
        self.ops = []
        self.last_write = {}
        self.readers = {}
        self.sem_keys = set()
        self.know = {}
        self.snap = {}
        self.dma_count = {}
        self.last_dma = {}
        self.relay_cover = {}
        self.relay_last = {}

    def _K(self, eng):
        return self.know.setdefault(eng, {})

    def _ensure_relay(self, q, need_ord):
        if self.relay_cover.get(q, -1) >= need_ord:
            return self.relay_last[q]
        idx = len(self.ops)
        val = 16 * self.dma_count.get(q, 0)
        self.ops.append(dict(eng=q, sem=q + "r", fn=None, relay_wait=val,
                             waits=[], needs_inc=True, inc_amt=1, tail=False,
                             dma=False))
        self.sem_keys.add(q + "r")
        self.snap[idx] = dict(self._K(q))
        self.relay_cover[q] = self.dma_count.get(q, 0)
        self.relay_last[q] = idx
        return idx

    def _edge(self, eng, K, d):
        o = self.ops[d]
        if o["dma"]:
            q = o["eng"]
            if eng == q:
                d = self.last_dma[q]
            else:
                d = self._ensure_relay(q, o["dma_ord"])
            o = self.ops[d]
        de = o["eng"]
        key = o["sem"]
        if de == eng and o["sem"] == de and not PARANOID and not o["tail"]:
            return None
        if K.get(key, -1) >= d:
            return None
        return d

    def _wait_on(self, eng, K, d, waits):
        o = self.ops[d]
        o["needs_inc"] = True
        waits.append(d)
        for f, v in self.snap.get(d, {}).items():
            if K.get(f, -1) < v:
                K[f] = v
        K[o["sem"]] = max(K.get(o["sem"], -1), d)

    def op(self, eng, fn, r=(), w=(), dma=False, tail=False, coll=False):
        deps = set()
        for t in r:
            d = self.last_write.get(t)
            if d is not None:
                deps.add(d)
        for t in w:
            d = self.last_write.get(t)
            if d is not None:
                deps.add(d)
            for d2 in self.readers.get(t, ()):
                deps.add(d2)
        K = self._K(eng)
        waits = []
        for d in sorted(deps):
            d2 = self._edge(eng, K, d)
            if d2 is not None:
                self._wait_on(eng, K, d2, waits)
        if coll:
            sem = eng + "c"
        elif dma:
            sem = eng
        else:
            sem = eng if eng in ("pe", "dve", "act") else eng + "e"
        idx = len(self.ops)
        self.ops.append(dict(eng=eng, sem=sem, fn=fn, waits=waits,
                             needs_inc=dma or coll, inc_amt=16 if dma else 1,
                             tail=tail, dma=dma))
        self.sem_keys.add(sem)
        self.snap[idx] = dict(K)
        if dma:
            self.dma_count[eng] = self.dma_count.get(eng, 0) + 1
            self.ops[idx]["dma_ord"] = self.dma_count[eng]
            self.last_dma[eng] = idx
        for t in w:
            self.last_write[t] = idx
            self.readers[t] = []
        for t in r:
            self.readers.setdefault(t, []).append(idx)
        return idx

    def barrier(self):
        latest = {}
        for i, o in enumerate(self.ops):
            if o["fn"] is not None or o.get("relay_wait") is not None:
                latest[o["eng"]] = i
        for eng in ENGINES:
            K = self._K(eng)
            waits = []
            for f, d in latest.items():
                if f == eng:
                    continue
                d2 = self._edge(eng, K, d)
                if d2 is not None:
                    self._wait_on(eng, K, d2, waits)
            if waits:
                idx = len(self.ops)
                self.ops.append(dict(eng=eng, sem=eng, fn=None, waits=waits,
                                     needs_inc=False, inc_amt=1, tail=False,
                                     dma=False))
                self.snap[idx] = dict(K)

    def finalize(self):
        cum = {}
        self.target = {}
        for i, o in enumerate(self.ops):
            if o["needs_inc"]:
                cum[o["sem"]] = cum.get(o["sem"], 0) + o["inc_amt"]
                self.target[i] = cum[o["sem"]]


def _quant_plan(p, src_ap, out_ap, nblk, tm, src_toks, out_tok, pre_ap=None,
                pre_toks=(), npart=128, amax_eng="dve", dq_eng="dve"):
    n16 = nblk * 16
    A, B, C = tm["A"], tm["B"], tm["C"]
    E, F, G = tm.get("E"), tm.get("F"), tm["G"]
    if tm.get("swap_ef"):
        E, F = F, E
    lean = out_ap is not None
    am, scl, rs, rr = tm["am"], tm["scl"], tm["rs"], tm["rr"]
    src3 = src_ap.rearrange("p (s n) -> p s n", n=16)
    pfx = tm.get("pfx", "q")
    tA, tB, tC, tE, tF, tG, tS = ((pfx + "A",), (pfx + "B",), (pfx + "C",),
                                  (pfx + "E",), (pfx + "F",), (pfx + "G",),
                                  (pfx + "sm",))
    if tm.get("swap_ef"):
        tE, tF = tF, tE
    if out_ap is None:
        out_ap = E[0:npart, 0:n16]
        out_tok = tE
    out3 = out_ap.rearrange("p (s n) -> p s n", n=16)

    p.op(amax_eng, lambda v: v.tensor_reduce(am[0:npart, 0:nblk].unsqueeze(2), src3,
                                             AX.X, ALU.max, apply_absolute_value=True),
         r=list(src_toks) + [tS], w=[tS], tail=True)
    if pre_ap is not None:
        p.op("dve", lambda v: v.tensor_scalar(am[0:npart, 0:nblk], am[0:npart, 0:nblk], pre_ap,
                                              None, ALU.mult),
             r=[tS] + list(pre_toks), w=[tS])
    p.op("dve", lambda v: v.tensor_scalar(scl[0:npart, 0:nblk], am[0:npart, 0:nblk], 1.0 / 6.0, 1e-12,
                                          ALU.mult, ALU.max), r=[tS], w=[tS])
    p.op("dve", lambda v: v.reciprocal(rs[0:npart, 0:nblk], scl[0:npart, 0:nblk]), r=[tS], w=[tS])
    if pre_ap is not None:
        p.op("dve", lambda v: v.tensor_scalar(rr[0:npart, 0:nblk], rs[0:npart, 0:nblk], pre_ap,
                                              None, ALU.mult),
             r=[tS] + list(pre_toks), w=[tS])
        rsv = rr
    else:
        rsv = rs
    p.op("dve", lambda v: v.tensor_tensor(
        A[0:npart, 0:n16].rearrange("p (s n) -> p s n", n=16), src3,
        rsv[0:npart, 0:nblk].unsqueeze(2).broadcast_to([npart, nblk, 16]), ALU.mult),
        r=list(src_toks) + [tS, tA], w=[tA])
    p.op("dve", lambda v: v.tensor_scalar(B[0:npart, 0:n16], A[0:npart, 0:n16], -2.0, 2.0,
                                          ALU.max, ALU.min), r=[tA, tB], w=[tB])
    p.op("dve", lambda v: v.tensor_scalar(C[0:npart, 0:n16], A[0:npart, 0:n16], -4.0, 4.0,
                                          ALU.max, ALU.min), r=[tA, tC], w=[tC])
    p.op("dve", lambda v: v.tensor_scalar(G[0:npart, 0:n16], B[0:npart, 0:n16], RC1, RC1,
                                          ALU.add, ALU.subtract), r=[tB, tG], w=[tG])
    p.op("dve", lambda v: v.scalar_tensor_tensor(B[0:npart, 0:n16], C[0:npart, 0:n16], RC2,
                                                 B[0:npart, 0:n16], ALU.add, ALU.subtract),
         r=[tB, tC], w=[tB])
    p.op("dve", lambda v: v.scalar_tensor_tensor(A[0:npart, 0:n16], A[0:npart, 0:n16], RC3,
                                                 C[0:npart, 0:n16], ALU.add, ALU.subtract),
         r=[tA, tC], w=[tA])
    if lean:
        p.op("dve", lambda v: v.scalar_tensor_tensor(B[0:npart, 0:n16], B[0:npart, 0:n16],
                                                     -RC2, G[0:npart, 0:n16],
                                                     ALU.add, ALU.add), r=[tB, tG], w=[tB])
        p.op("dve", lambda v: v.scalar_tensor_tensor(B[0:npart, 0:n16], A[0:npart, 0:n16],
                                                     -RC3, B[0:npart, 0:n16],
                                                     ALU.add, ALU.add), r=[tA, tB], w=[tB])
        last, ltok = B, tB
    else:
        p.op("dve", lambda v: v.scalar_tensor_tensor(E[0:npart, 0:n16], B[0:npart, 0:n16],
                                                     -RC2, G[0:npart, 0:n16],
                                                     ALU.add, ALU.add), r=[tB, tG, tE], w=[tE])
        p.op("dve", lambda v: v.scalar_tensor_tensor(E[0:npart, 0:n16], A[0:npart, 0:n16],
                                                     -RC3, E[0:npart, 0:n16],
                                                     ALU.add, ALU.add), r=[tA, tE], w=[tE])
        last, ltok = E, tE
    p.op(dq_eng, lambda v: v.tensor_tensor(
        out3, last[0:npart, 0:n16].rearrange("p (s n) -> p s n", n=16),
        scl[0:npart, 0:nblk].unsqueeze(2).broadcast_to([npart, nblk, 16]), ALU.mult),
        r=[ltok, tS, out_tok], w=[out_tok])


def build_nc(debug=False):
    nc = bass.Bass("TRN2", target_bir_lowering=False)
    x_h = nc.dram_tensor("x", [T, D], F32, kind="ExternalInput").ap()
    wq_h = nc.dram_tensor("wq", [D, D], F32, kind="ExternalInput").ap()
    wk_h = nc.dram_tensor("wk", [D, D], F32, kind="ExternalInput").ap()
    wv_h = nc.dram_tensor("wv", [D, D], F32, kind="ExternalInput").ap()
    wo_h = nc.dram_tensor("wo", [D, D], F32, kind="ExternalInput").ap()
    bo_h = nc.dram_tensor("bo", [D], F32, kind="ExternalInput").ap()
    wgs = [nc.dram_tensor(f"wgs{h}", [IH // 8, D], F32, kind="ExternalInput").ap()
           for h in range(2)]
    wus = [nc.dram_tensor(f"wus{h}", [IH // 8, D], F32, kind="ExternalInput").ap()
           for h in range(2)]
    wds = [nc.dram_tensor(f"wds{h}", [D // 8, IH], F32, kind="ExternalInput").ap()
           for h in range(2)]
    wall_ci = [nc.dram_tensor(f"wall_ci{h}", [528, D], BF16).ap() for h in range(2)]
    wall_co = [nc.dram_tensor(f"wall_co{h}", [8 * 528, D], BF16, addr_space="Shared").ap()
               for h in range(2)]
    out_h = nc.dram_tensor("out", [T, D], F32, kind="ExternalOutput").ap()
    dbg = {}
    if debug:
        dbg["kv"] = nc.dram_tensor("dbg_kv", [128, 8 * 129], F32, kind="ExternalOutput").ap()
        dbg["q"] = nc.dram_tensor("dbg_q", [128, 8 * T], BF16, kind="ExternalOutput").ap()
        dbg["xqT"] = nc.dram_tensor("dbg_xqT", [128, 8 * T], BF16, kind="ExternalOutput").ap()

    from contextlib import ExitStack
    p = Planner()
    with ExitStack() as gctx:
        g = gctx.enter_context
        iot = g(nc.sbuf_tensor("iot", [128, 128], F32))
        identf = g(nc.sbuf_tensor("identf", [128, 128], F32))
        identb = g(nc.sbuf_tensor("identb", [128, 128], BF16))
        m_all = g(nc.sbuf_tensor("m_all", [128, NTL], F32))
        negm = g(nc.sbuf_tensor("negm", [128, NTL], F32))
        rstd = g(nc.sbuf_tensor("rstd", [128, NTL], F32))
        vtmp = g(nc.sbuf_tensor("vtmp", [128, NTL], F32))

        p.op("gp", lambda e: e.iota(iot[:], [[1, 128]], channel_multiplier=-1,
                                    allow_small_or_imprecise_dtypes=True), w=[("iot",)])
        p.op("dve", lambda v: v.tensor_single_scalar(identf[:], iot[:], 0.0, ALU.is_equal),
             r=[("iot",)], w=[("identf",)])
        p.op("dve", lambda v: v.tensor_copy(identb[:], identf[:]),
             r=[("identf",)], w=[("identb",)])

        qs_ctx = ExitStack()
        gctx.enter_context(qs_ctx)
        eq = qs_ctx.enter_context
        qA = eq(nc.sbuf_tensor("qA", [128, IH], F32))
        qB = eq(nc.sbuf_tensor("qB", [128, IH], F32))
        qC = eq(nc.sbuf_tensor("qC", [128, IH], F32))
        qG = eq(nc.sbuf_tensor("qG", [128, IH], BF16))
        qam = eq(nc.sbuf_tensor("qam", [128, IH // 16], F32))
        qscl = eq(nc.sbuf_tensor("qscl", [128, IH // 16], F32))
        qrs = eq(nc.sbuf_tensor("qrs", [128, IH // 16], F32))
        qrr = eq(nc.sbuf_tensor("qrr", [128, IH // 16], F32))
        qtm = dict(A=qA, B=qB, C=qC, G=qG,
                   am=qam, scl=qscl, rs=qrs, rr=qrr)
        qxA = eq(nc.sbuf_tensor("qxA", [128, D], F32))
        qxB = eq(nc.sbuf_tensor("qxB", [128, D], F32))
        qxC = eq(nc.sbuf_tensor("qxC", [128, D], F32))
        qxE = eq(nc.sbuf_tensor("qxE", [128, D], BF16))
        qxF = eq(nc.sbuf_tensor("qxF", [128, D], BF16))
        qxG = eq(nc.sbuf_tensor("qxG", [128, D], BF16))
        qxam = eq(nc.sbuf_tensor("qxam", [128, D // 16], F32))
        qxscl = eq(nc.sbuf_tensor("qxscl", [128, D // 16], F32))
        qxrs = eq(nc.sbuf_tensor("qxrs", [128, D // 16], F32))
        qxrr = eq(nc.sbuf_tensor("qxrr", [128, D // 16], F32))
        qtm2 = dict(A=qxA, B=qxB, C=qxC, E=qxE, F=qxF, G=qxG,
                    am=qxam, scl=qxscl, rs=qxrs, rr=qxrr, pfx="x")
        ssum = eq(nc.sbuf_tensor("ssum", [128, 1], F32))
        ssq = eq(nc.sbuf_tensor("ssq", [128, 1], F32))
        nrt1 = eq(nc.sbuf_tensor("nrt1", [128, 1], F32))
        xqT = eq(nc.sbuf_tensor("xqT", [128, 8 * T], BF16))

        if "a1" in PHASES or "a2" in PHASES:
            with ExitStack() as s:
                e = s.enter_context
                wshp = [e(nc.sbuf_tensor(f"wsh{i}", [128, IH], F32)) for i in range(2)]
                wshqp = [e(nc.sbuf_tensor(f"wshq{i}", [128, IH], BF16)) for i in range(2)]
                pc = 0
                for hf in range(2):
                    jobs = [(wgs[hf], 0, [(0, 128, D), (128, 48, D)]),
                            (wus[hf], 176, [(0, 128, D), (128, 48, D)]),
                            (wds[hf], 352, [(0, 128, IH)])]
                    for src_h, base, pieces in jobs:
                        for (r0, npp, ncols) in pieces:
                            wsh, wshq = wshp[pc % 2], wshqp[pc % 2]
                            wt, wqt_ = ("wsh", pc % 2), ("wshq", pc % 2)
                            pc += 1
                            p.op("sp", lambda e_, src_h=src_h, r0=r0, npp=npp, ncols=ncols,
                                 wsh=wsh: e_.dma_start(out=wsh[0:npp, 0:ncols],
                                                       in_=src_h[r0:r0 + npp, :]),
                                 r=[wt], w=[wt], dma=True)
                            _quant_plan(p, wsh[0:npp, 0:ncols], wshq[0:npp, 0:ncols],
                                        ncols // 16, qtm, [wt], wqt_,
                                        npart=npp)
                            flat = wall_ci[hf].flatten()
                            o0 = (base + r0) * D
                            dst = flat[o0: o0 + npp * ncols].rearrange(
                                "(p n) -> p n", n=ncols)
                            p.op("sp", lambda e_, dst=dst, npp=npp, ncols=ncols,
                                 wshq=wshq: e_.dma_start(out=dst,
                                                         in_=wshq[0:npp, 0:ncols]),
                                 r=[wqt_], w=[("wci", hf), wqt_], dma=True)
                    p.op("gp", lambda g_, hf=hf: g_.collective_compute(
                        "AllGather", ALU.bypass,
                        replica_groups=[list(range(8))],
                        ins=[wall_ci[hf].opt()], outs=[wall_co[hf].opt()]),
                        r=[("wci", hf), ("ccq",)],
                        w=[("wgath", hf), ("ccq",)], coll=True)

        if "a0" in PHASES:
            with ExitStack() as s:
                e = s.enter_context
                x_sb = [e(nc.sbuf_tensor(f"a0x{i}", [128, D], F32)) for i in range(3)]
                sq = e(nc.sbuf_tensor("a0sq", [128, D], F32))
                ssum = e(nc.sbuf_tensor("a0ssum", [128, 1], F32))
                ssq = e(nc.sbuf_tensor("a0ssq", [128, 1], F32))
                for tt in range(NTL):
                    xb = x_sb[tt % 3]
                    xt = ("a0x", tt % 3)
                    p.op("sp", lambda e_, tt=tt, xb=xb: e_.dma_start(
                        out=xb[:], in_=x_h[tt * 128:(tt + 1) * 128, :]),
                        w=[xt], dma=True)
                    p.op("dve", lambda v, xb=xb: v.reduce_sum(ssum[:], xb[:], AX.X),
                         r=[xt], w=[("a0s",)], tail=True)
                    p.op("dve", lambda v, tt=tt: v.tensor_scalar(
                        m_all[:, tt:tt + 1], ssum[:], 1.0 / D, None, ALU.mult),
                        r=[("a0s",)], w=[("m", tt)])
                    p.op("act", lambda a, xb=xb: a.activation(sq[:], xb[:], AF.Square,
                                                              accum_out=ssq[:]),
                         r=[xt], w=[("a0sq",), ("a0ssq",)], tail=True)
                    p.op("dve", lambda v, tt=tt: v.tensor_tensor(
                        vtmp[:, tt:tt + 1], m_all[:, tt:tt + 1], m_all[:, tt:tt + 1],
                        ALU.mult), r=[("m", tt)], w=[("vt", tt)])
                    p.op("dve", lambda v, tt=tt: v.tensor_scalar(
                        vtmp[:, tt:tt + 1], vtmp[:, tt:tt + 1], LN_EPS, None,
                        ALU.subtract), r=[("vt", tt)], w=[("vt", tt)])
                    p.op("dve", lambda v, tt=tt: v.scalar_tensor_tensor(
                        vtmp[:, tt:tt + 1], ssq[:], 1.0 / D, vtmp[:, tt:tt + 1],
                        ALU.mult, ALU.subtract),
                        r=[("a0ssq",), ("vt", tt)], w=[("vt", tt)])
                    p.op("act", lambda a, tt=tt: a.activation(
                        vtmp[:, tt:tt + 1], vtmp[:, tt:tt + 1], AF.Sqrt),
                        r=[("vt", tt)], w=[("vt", tt)])
                    p.op("dve", lambda v, tt=tt: v.reciprocal(
                        rstd[:, tt:tt + 1], vtmp[:, tt:tt + 1]),
                        r=[("vt", tt)], w=[("rstd", tt)])
                    p.op("dve", lambda v, tt=tt: v.tensor_scalar(
                        negm[:, tt:tt + 1], m_all[:, tt:tt + 1], -1.0, None, ALU.mult),
                        r=[("m", tt)], w=[("negm", tt)])
                p.barrier()

        if True:
            def plan_ffn_half(hf):
                with ExitStack() as s:
                    e = s.enter_context
                    WgT = e(nc.sbuf_tensor(f"WgT{hf}", [128, 8 * IH], BF16))
                    WuT = e(nc.sbuf_tensor(f"WuT{hf}", [128, 8 * IH], BF16))
                    WdT = e(nc.sbuf_tensor(f"WdT{hf}", [128, 11 * D], BF16))
                    wqt = [e(nc.sbuf_tensor(f"wqt{hf}_{i}", [128, IH], BF16))
                           for i in range(2)]
                    trp = [e(nc.psum_tensor(f"wtr{hf}_{i}", [128, 1024], BF16))
                           for i in range(2)]

                    wco_flat = wall_co[hf].flatten()

                    def load_w(base, dstT, ntiles, ncols, n_tr, off_fn, dtok):
                        per = 176 if ncols == D else 128
                        for r in range(ntiles):
                            wq_ = wqt[r % 2]
                            wtok = ("wqt", hf, r % 2)
                            p0 = 0
                            while p0 < 128:
                                g = r * 128 + p0
                                rank, loc = g // per, g % per
                                n = min(128 - p0, per - loc)
                                if ncols == D:
                                    o0 = (rank * 528 + base + loc) * D
                                else:
                                    o0 = (rank * 528 + base) * D + loc * ncols
                                ap_in = wco_flat[o0: o0 + n * ncols].rearrange(
                                    "(p n) -> p n", n=ncols)
                                p.op("sp", lambda e_, ap_in=ap_in, p0=p0, n=n,
                                     ncols=ncols, wq_=wq_:
                                     e_.dma_start(out=wq_[p0:p0 + n, 0:ncols], in_=ap_in),
                                     r=[wtok, ("wgath", hf)], w=[wtok], dma=True)
                                p0 += n
                            for k in range(n_tr):
                                ps = trp[k % 2]
                                pstok = ("trp", hf, k % 2)
                                p.op("pe", lambda t_, k=k, ps=ps, wq_=wq_: t_.transpose(
                                    ps[:, (k % 8) * 128:(k % 8) * 128 + 128],
                                    wq_[:, k * 128:(k + 1) * 128], identb[:]),
                                    r=[wtok, ("identb",)], w=[pstok])
                                off = off_fn(r, k)
                                p.op("act", lambda a, ps=ps, k=k, off=off, dstT=dstT:
                                     a.activation(dstT[:, off:off + 128],
                                                  ps[:, (k % 8) * 128:(k % 8) * 128 + 128],
                                                  AF.Copy),
                                     r=[pstok], w=[dtok])

                    def load_w_all():
                        load_w(0, WgT, 11, D, 8,
                               lambda r, k: k * IH + r * 128, ("WgT", hf))
                        load_w(176, WuT, 11, D, 8,
                               lambda r, k: k * IH + r * 128, ("WuT", hf))
                        load_w(352, WdT, 8, IH, 11,
                               lambda r, k: k * D + r * 128, ("WdT", hf))

                    res_sb = [e(nc.sbuf_tensor(f"ars{hf}_{i}", [128, D], F32)) for i in range(2)]
                    sg_sb = e(nc.sbuf_tensor(f"asg{hf}", [128, 512], F32))
                    hqT_sb = [e(nc.sbuf_tensor(f"ahqT{hf}_{i}", [128, 11 * 128], BF16))
                              for i in range(2)]
                    hqo = [e(nc.sbuf_tensor(f"ahqo{hf}_{i}", [128, IH], BF16))
                           for i in range(2)]
                    outF = [e(nc.sbuf_tensor(f"aoF{hf}_{i}", [128, D], F32)) for i in range(2)]
                    gps = [e(nc.psum_tensor(f"agps{hf}_{i}", [128, 512], F32)) for i in range(2)]
                    ups = [e(nc.psum_tensor(f"aups{hf}_{i}", [128, 512], F32)) for i in range(2)]
                    dnps = [e(nc.psum_tensor(f"adn{hf}_{i}", [128, 512], F32)) for i in range(2)]

                    def stage1(tt):
                        p.op("sp", lambda e_, tt=tt: e_.dma_start(
                            out=qxA[:, 0:D], in_=x_h[tt * 128:(tt + 1) * 128, :]),
                            r=[("xA",)], w=[("xA",)], dma=True)
                        p.op("act", lambda a, tt=tt: a.activation(
                            qxA[:, 0:D], qxA[:, 0:D], AF.Identity, bias=negm[:, tt:tt + 1]),
                            r=[("xA",), ("negm", tt)], w=[("xA",)])
                        _quant_plan(p, qxA[:, 0:D], None, D // 16,
                                    dict(qtm2, swap_ef=(tt % 2 == 1)),
                                    [("xA",)], None,
                                    pre_ap=rstd[:, tt:tt + 1], pre_toks=[("rstd", tt)])

                    def stage2(tt):
                        xq = qxE if tt % 2 == 0 else qxF
                        xtok = ("xE",) if tt % 2 == 0 else ("xF",)
                        ch, j = tt // 4, tt % 4
                        for k in range(8):
                            ps = trp[k % 2]
                            pstok = ("trp", hf, k % 2)
                            p.op("pe", lambda t_, k=k, ps=ps, xq=xq: t_.transpose(
                                ps[:, (k % 8) * 128:(k % 8) * 128 + 128],
                                xq[:, k * 128:(k + 1) * 128], identb[:]),
                                r=[xtok, ("identb",)], w=[pstok])
                            off = k * T + ch * CH + j * 128
                            p.op("act", lambda a, ps=ps, k=k, off=off: a.activation(
                                xqT[:, off:off + 128],
                                ps[:, (k % 8) * 128:(k % 8) * 128 + 128], AF.Copy),
                                r=[pstok], w=[("xqT", tt)])

                    IC_BLKS = ((0, 512), (512, 512), (1024, 384))

                    def plan_gu(tt, ic):
                        i0, iw = IC_BLKS[ic]
                        gp_, up_ = gps[ic % 2], ups[ic % 2]
                        gt, ut = ("gps", hf, ic % 2), ("ups", hf, ic % 2)

                        def mm_gateup(t_, tt=tt, i0=i0, iw=iw, gp_=gp_, up_=up_):
                            last = None
                            for k in range(8):
                                lhs = xqT[:, k * T + tt * 128: k * T + tt * 128 + 128]
                                t_.matmul(gp_[:, 0:iw], lhs,
                                          WgT[:, k * IH + i0: k * IH + i0 + iw],
                                          start=(k == 0), stop=(k == 7))
                                last = t_.matmul(up_[:, 0:iw], lhs,
                                                 WuT[:, k * IH + i0: k * IH + i0 + iw],
                                                 start=(k == 0), stop=(k == 7))
                            return last
                        p.op("pe", mm_gateup,
                             r=[("xqT", tt), ("WgT", hf), ("WuT", hf)],
                             w=[gt, ut])

                    def plan_sh(tt, ic):
                        i0, iw = IC_BLKS[ic]
                        gp_, up_ = gps[ic % 2], ups[ic % 2]
                        gt, ut = ("gps", hf, ic % 2), ("ups", hf, ic % 2)
                        p.op("act", lambda a, gp_=gp_, iw=iw: a.activation(
                            sg_sb[:, 0:iw], gp_[:, 0:iw], AF.Silu),
                            r=[gt, ("sg", hf)], w=[("sg", hf)])
                        p.op("dve", lambda v, up_=up_, i0=i0, iw=iw: v.tensor_tensor(
                            qA[:, i0:i0 + iw], sg_sb[:, 0:iw], up_[:, 0:iw],
                            ALU.mult),
                            r=[("sg", hf), ut, ("qA",)], w=[("qA",)])

                    def stage3a(tt):
                        rb = res_sb[tt % 2]
                        rtok = ("ars", hf, tt % 2)
                        if hf == 0:
                            p.op("sp", lambda e_, tt=tt, rb=rb: e_.dma_start(
                                out=rb[:], in_=x_h[tt * 128:(tt + 1) * 128, :]),
                                r=[rtok], w=[rtok], dma=True)
                        else:
                            p.op("sp", lambda e_, tt=tt, rb=rb: e_.dma_start(
                                out=rb[:], in_=out_h[tt * 128:(tt + 1) * 128, :]),
                                r=[rtok, ("out_h", tt)], w=[rtok], dma=True)
                        plan_gu(tt, 0)

                    def stage3b(tt):
                        plan_sh(tt, 0)
                        plan_gu(tt, 1)
                        plan_sh(tt, 1)
                        plan_gu(tt, 2)
                        plan_sh(tt, 2)
                        _quant_plan(p, qA[:], hqo[tt % 2][:], IH // 16,
                                    dict(qtm, swap_ef=(tt % 2 == 1)),
                                    [("qA",)], ("hqo", tt % 2))

                    def stage3(tt):
                        stage3a(tt)
                        stage3b(tt)

                    def stage4(tt):
                        hq = hqo[tt % 2]
                        htok = ("hqo", tt % 2)
                        rb = res_sb[tt % 2]
                        rtok = ("ars", hf, tt % 2)
                        hqT = hqT_sb[tt % 2]
                        for k in range(11):
                            ps = trp[k % 2]
                            pstok = ("trp", hf, k % 2)
                            p.op("pe", lambda t_, k=k, ps=ps, hq=hq: t_.transpose(
                                ps[:, (k % 8) * 128:(k % 8) * 128 + 128],
                                hq[:, k * 128:(k + 1) * 128], identb[:]),
                                r=[htok, ("identb",)], w=[pstok])
                            p.op("act", lambda a, ps=ps, k=k, hqT=hqT: a.activation(
                                hqT[:, k * 128:(k + 1) * 128],
                                ps[:, (k % 8) * 128:(k % 8) * 128 + 128], AF.Copy),
                                r=[pstok], w=[("hqT", hf, tt % 2)])
                        oF = outF[tt % 2]
                        for dh2 in range(2):
                            dn = dnps[dh2]
                            dtok = ("dn", hf, dh2)

                            def mm_down(t_, hqT=hqT, dh2=dh2, dn=dn):
                                last = None
                                for k in range(11):
                                    last = t_.matmul(
                                        dn[:], hqT[:, k * 128:(k + 1) * 128],
                                        WdT[:, k * D + dh2 * 512: k * D + dh2 * 512 + 512],
                                        start=(k == 0), stop=(k == 10))
                                return last
                            p.op("pe", mm_down,
                                 r=[("hqT", hf, tt % 2), ("WdT", hf)], w=[dtok])
                            p.op("dve", lambda v, dn=dn, dh2=dh2, oF=oF, rb=rb:
                                 v.tensor_tensor(oF[:, dh2 * 512:(dh2 + 1) * 512], dn[:],
                                                 rb[:, dh2 * 512:(dh2 + 1) * 512], ALU.add),
                                 r=[dtok, rtok, ("oF", hf, tt % 2)],
                                 w=[("oF", hf, tt % 2)])
                        p.op("sp", lambda e_, tt=tt, oF=oF: e_.dma_start(
                            out=out_h[tt * 128:(tt + 1) * 128, :], in_=oF[:]),
                            r=[("oF", hf, tt % 2)], w=[("out_h", tt)], dma=True)

                    if hf == 0:
                        PRO = 3
                        for i in range(NTL + PRO + 2):
                            if 0 <= i - PRO < NTL:
                                stage3a(i - PRO)
                            if i < NTL:
                                stage1(i)
                            if 0 <= i - 1 < NTL:
                                stage2(i - 1)
                            if i == PRO - 1:
                                load_w_all()
                            if 0 <= i - PRO < NTL:
                                stage3b(i - PRO)
                            if 0 <= i - PRO - 1 < NTL:
                                stage4(i - PRO - 1)
                    else:
                        load_w_all()
                        for i in range(NTL + 1):
                            if i < NTL:
                                stage3(i)
                            if 0 <= i - 1 < NTL:
                                stage4(i - 1)
                    p.barrier()
                    p.barrier()

            if "a1" in PHASES:
                plan_ffn_half(0)
            if debug:
                p.op("sp", lambda e_: e_.dma_start(out=dbg["xqT"], in_=xqT[:]),
                     r=[("xqT", tt2) for tt2 in range(NTL)], w=[], dma=True)
            if "a2" in PHASES:
                plan_ffn_half(1)
            p.barrier()

        qs_ctx.close()

        if "b1" in PHASES or "b2" in PHASES:
            with ExitStack() as bs:
                eb = bs.enter_context
                q_full = eb(nc.sbuf_tensor("q_full", [128, 8 * T], BF16))
                kv_acc = eb(nc.sbuf_tensor("kv_acc", [128, 8 * 129], F32))
                p.op("dve", lambda v: v.memset(kv_acc[:], 0.0), w=[("kv_acc",)])

                if "b1" in PHASES:
                    with ExitStack() as s:
                        e = s.enter_context
                        wqb = e(nc.sbuf_tensor("wqb", [128, 8 * D], BF16))
                        wkb = e(nc.sbuf_tensor("wkb", [128, 8 * D], BF16))
                        wvb = e(nc.sbuf_tensor("wvb", [128, 8 * D], BF16))
                        x_sb = [e(nc.sbuf_tensor(f"bx{i}", [128, D], F32)) for i in range(2)]
                        xn_sb = [e(nc.sbuf_tensor(f"bxn{i}", [128, D], BF16)) for i in range(2)]
                        xnT = [e(nc.sbuf_tensor(f"bxnT{c}", [128, 8 * CH], BF16))
                               for c in range(2)]
                        k_sb = [e(nc.sbuf_tensor(f"bk{i}", [128, D], BF16)) for i in range(8)]
                        v_sb = [e(nc.sbuf_tensor(f"bv{i}", [128, 8 * 130], BF16))
                                for i in range(8)]
                        m1_sb = e(nc.sbuf_tensor("bm1", [128, 512], F32))
                        e1_sb = e(nc.sbuf_tensor("be1", [128, 512], F32))
                        trp = [e(nc.psum_tensor(f"btr{i}", [128, 1024], BF16)) for i in range(2)]
                        qps = [e(nc.psum_tensor(f"bqps{i}", [128, 512], F32)) for i in range(2)]
                        kvp = [e(nc.psum_tensor(f"bkv{i}", [128, 512], F32)) for i in range(2)]

                        for nm, hsrc, dst in (("wq", wq_h, wqb), ("wk", wk_h, wkb),
                                              ("wv", wv_h, wvb)):
                            for k in range(8):
                                p.op("gp", lambda e_, hsrc=hsrc, dst=dst, k=k: e_.dma_start(
                                    out=dst[:, k * D:(k + 1) * D],
                                    in_=hsrc[k * 128:(k + 1) * 128, :]),
                                    w=[(nm,)], dma=True)
                        for j in range(8):
                            p.op("dve", lambda v, j=j: v.memset(
                                v_sb[j][:].rearrange("p (a c) -> p a c", c=130)[:, :, 128:130],
                                1.0), w=[("vones", j)])

                        def b1_tiles(ch):
                            cp = ch % 2
                            xT = xnT[cp]
                            for j in range(4):
                                tt = ch * 4 + j
                                xb = x_sb[j % 2]
                                xtok = ("bx", j % 2)
                                p.op("sp", lambda e_, tt=tt, xb=xb: e_.dma_start(
                                    out=xb[:], in_=x_h[tt * 128:(tt + 1) * 128, :]),
                                    r=[xtok], w=[xtok], dma=True)
                                xn = xn_sb[j % 2]
                                xntok = ("bxn", j % 2)
                                p.op("dve", lambda v, tt=tt, xb=xb, xn=xn:
                                     v.scalar_tensor_tensor(
                                         xn[:], xb[:], m_all[:, tt:tt + 1],
                                         rstd[:, tt:tt + 1].broadcast_to([128, D]),
                                         ALU.subtract, ALU.mult),
                                     r=[xtok, ("m", tt), ("rstd", tt), xntok], w=[xntok])
                                for k in range(8):
                                    ps = trp[k % 2]
                                    pstok = ("btrp", k % 2)
                                    p.op("pe", lambda t_, k=k, ps=ps, xn=xn: t_.transpose(
                                        ps[:, (k % 8) * 128:(k % 8) * 128 + 128],
                                        xn[:, k * 128:(k + 1) * 128], identb[:]),
                                        r=[xntok, ("identb",)], w=[pstok])
                                    p.op("act", lambda a, ps=ps, k=k, j=j, xT=xT: a.activation(
                                        xT[:, k * CH + j * 128: k * CH + j * 128 + 128],
                                        ps[:, (k % 8) * 128:(k % 8) * 128 + 128], AF.Copy),
                                        r=[pstok], w=[("xnT", cp, k, j)])
                                for fh in range(2):
                                    kp = qps[fh]
                                    ktok = ("bqps", fh)
                                    jb = cp * 4 + j

                                    def mm_k(t_, j=j, fh=fh, kp=kp, xT=xT, W=wkb):
                                        last = None
                                        for k in range(8):
                                            last = t_.matmul(
                                                kp[:],
                                                xT[:, k * CH + j * 128: k * CH + j * 128 + 128],
                                                W[:, k * D + fh * 512: k * D + fh * 512 + 512],
                                                start=(k == 0), stop=(k == 7))
                                        return last
                                    p.op("pe", mm_k,
                                         r=[("xnT", cp, k, j) for k in range(8)] + [("wk",)],
                                         w=[ktok])
                                    p.op("dve", lambda v, kp=kp: v.tensor_scalar(
                                        m1_sb[:], kp[:], 0.0, None, ALU.min),
                                        r=[ktok, ("bm1",)], w=[("bm1",)])
                                    p.op("act", lambda a: a.activation(e1_sb[:], m1_sb[:],
                                                                       AF.Exp),
                                         r=[("bm1",), ("be1",)], w=[("be1",)])
                                    p.op("dve", lambda v, kp=kp, jb=jb, fh=fh:
                                         v.scalar_tensor_tensor(
                                             k_sb[jb][:, fh * 512:(fh + 1) * 512], kp[:],
                                             0.0, e1_sb[:], ALU.max, ALU.add),
                                         r=[ktok, ("be1",), ("bk", jb)], w=[("bk", jb)])

                                    def mm_v(t_, j=j, fh=fh, kp=kp, xT=xT, W=wvb):
                                        last = None
                                        for k in range(8):
                                            last = t_.matmul(
                                                kp[:],
                                                xT[:, k * CH + j * 128: k * CH + j * 128 + 128],
                                                W[:, k * D + fh * 512: k * D + fh * 512 + 512],
                                                start=(k == 0), stop=(k == 7))
                                        return last
                                    p.op("pe", mm_v,
                                         r=[("xnT", cp, k, j) for k in range(8)] + [("wv",)],
                                         w=[ktok])
                                    p.op("act", lambda a, kp=kp, jb=jb, fh=fh: a.activation(
                                        v_sb[jb][:].rearrange("p (a c) -> p a c", c=130)
                                        [:, 4 * fh:4 * fh + 4, 0:128],
                                        kp[:].rearrange("p (a c) -> p a c", c=128), AF.Copy),
                                        r=[ktok, ("bv", jb)], w=[("bv", jb)])

                        def b1_qkv(ch):
                            cp = ch % 2
                            xT = xnT[cp]
                            xnT_toks = [("xnT", cp, k, j) for k in range(8) for j in range(4)]
                            for fm in range(8):
                                qp = qps[fm % 2]
                                qtok = ("bqps", fm % 2)

                                def mm_q(t_, fm=fm, qp=qp, xT=xT):
                                    last = None
                                    for k in range(8):
                                        last = t_.matmul(
                                            qp[:], wqb[:, k * D + fm * 128: k * D + fm * 128 + 128],
                                            xT[:, k * CH:(k + 1) * CH],
                                            start=(k == 0), stop=(k == 7))
                                    return last
                                p.op("pe", mm_q, r=xnT_toks + [("wq",)], w=[qtok])
                                p.op("dve", lambda v, qp=qp: v.tensor_scalar(
                                    m1_sb[:], qp[:], 0.0, None, ALU.min),
                                    r=[qtok, ("bm1",)], w=[("bm1",)])
                                p.op("act", lambda a: a.activation(e1_sb[:], m1_sb[:], AF.Exp),
                                     r=[("bm1",), ("be1",)], w=[("be1",)])
                                p.op("dve", lambda v, qp=qp, fm=fm, ch=ch:
                                     v.scalar_tensor_tensor(
                                         q_full[:, fm * T + ch * CH: fm * T + (ch + 1) * CH],
                                         qp[:], 0.0, e1_sb[:], ALU.max, ALU.add),
                                     r=[qtok, ("be1",)], w=[("q", ch, fm)])
                            for pr in range(8):
                                kvb = kvp[pr % 2]
                                kvtok = ("bkvp", pr % 2)

                                def mm_kv(t_, pr=pr, kvb=kvb, cp=cp):
                                    last = None
                                    for j in range(4):
                                        last = t_.matmul(
                                            kvb[:, 0:129],
                                            k_sb[cp * 4 + j][:, pr * 128:(pr + 1) * 128],
                                            v_sb[cp * 4 + j][:, pr * 130: pr * 130 + 129],
                                            start=(j == 0), stop=(j == 3))
                                    return last
                                p.op("pe", mm_kv,
                                     r=[("bk", cp * 4 + j) for j in range(4)]
                                       + [("bv", cp * 4 + j) for j in range(4)]
                                       + [("vones", cp * 4 + j) for j in range(4)],
                                     w=[kvtok])
                                p.op("dve", lambda v, pr=pr, kvb=kvb: v.tensor_tensor(
                                    kv_acc[:, pr * 129:(pr + 1) * 129],
                                    kv_acc[:, pr * 129:(pr + 1) * 129],
                                    kvb[:, 0:129], ALU.add),
                                    r=[kvtok, ("kv_acc",)], w=[("kv_acc",)])

                        for ch in range(NCH + 1):
                            if 0 <= ch - 1 < NCH:
                                b1_qkv(ch - 1)
                            if ch < NCH:
                                b1_tiles(ch)
                        p.barrier()

                if "b2" in PHASES:
                    with ExitStack() as s:
                        e = s.enter_context
                        wob = e(nc.sbuf_tensor("wob", [128, 8 * D], BF16))
                        bo_sb = e(nc.sbuf_tensor("bo_sb", [128, 8], F32))
                        kvD = e(nc.sbuf_tensor("kvD", [128, 8 * 128], BF16))
                        KSD = e(nc.sbuf_tensor("KSD", [128, 8 * 16], BF16))
                        rz_sb = e(nc.sbuf_tensor("rz", [16, 512], F32))
                        rzb = e(nc.sbuf_tensor("rzb", [16, 512], BF16))
                        SEL = e(nc.sbuf_tensor("SEL", [16, 8 * 128], BF16))
                        selt = e(nc.sbuf_tensor("selt", [16, 128], F32))
                        sel1 = e(nc.sbuf_tensor("sel1", [16, 128], F32))
                        sel2 = e(nc.sbuf_tensor("sel2", [16, 128], F32))
                        rzx = e(nc.sbuf_tensor("rzx", [128, 512], BF16))
                        qs_sb = e(nc.sbuf_tensor("qs", [128, 8 * 512], BF16))
                        attS = e(nc.sbuf_tensor("attS", [128, 8 * 512], BF16))
                        ya_sb = e(nc.sbuf_tensor("ya", [128, 8 * 512], F32))
                        o2_sb = [e(nc.sbuf_tensor(f"bo2_{i}", [128, D], F32)) for i in range(4)]
                        oF_sb = [e(nc.sbuf_tensor(f"boF{i}", [128, D], F32)) for i in range(2)]
                        zps = e(nc.psum_tensor("bzps", [128, 512], F32))
                        rzps = e(nc.psum_tensor("brzps", [128, 512], F32))
                        atps = [e(nc.psum_tensor(f"batp{i}", [128, 512], F32)) for i in range(2)]
                        yaps = [e(nc.psum_tensor(f"byap{i}", [128, 512], F32)) for i in range(2)]
                        ftp = [e(nc.psum_tensor(f"bftp{i}", [128, 512], F32)) for i in range(2)]

                        for k in range(8):
                            p.op("gp", lambda e_, k=k: e_.dma_start(
                                out=wob[:, k * D:(k + 1) * D],
                                in_=wo_h[k * 128:(k + 1) * 128, :]),
                                w=[("wo",)], dma=True)
                            p.op("gp", lambda e_, k=k: e_.dma_start(
                                out=bo_sb[:, k:k + 1],
                                in_=bo_h[k * 128:(k + 1) * 128].rearrange("(p o) -> p o", o=1)),
                                w=[("bo",)], dma=True)
                        p.op("dve", lambda v: v.memset(KSD[:], 0.0), w=[("KSD",)])
                        for pr in range(8):
                            p.op("act", lambda a, pr=pr: a.activation(
                                kvD[:, pr * 128:(pr + 1) * 128],
                                kv_acc[:, pr * 129: pr * 129 + 128], AF.Copy),
                                r=[("kv_acc",)], w=[("kvD", pr)])
                            p.op("dve", lambda v, pr=pr: v.memset(
                                kvD[0:64, pr * 128 + 64:(pr + 1) * 128], 0.0),
                                r=[("kvD", pr)], w=[("kvD", pr)])
                            p.op("dve", lambda v, pr=pr: v.memset(
                                kvD[64:128, pr * 128: pr * 128 + 64], 0.0),
                                r=[("kvD", pr)], w=[("kvD", pr)])
                            p.op("act", lambda a, pr=pr: a.activation(
                                KSD[0:64, pr * 16 + 2 * pr: pr * 16 + 2 * pr + 1],
                                kv_acc[0:64, pr * 129 + 128: pr * 129 + 129], AF.Copy),
                                r=[("kv_acc",), ("KSD",)], w=[("KSD",)])
                            p.op("act", lambda a, pr=pr: a.activation(
                                KSD[64:128, pr * 16 + 2 * pr + 1: pr * 16 + 2 * pr + 2],
                                kv_acc[64:128, pr * 129 + 128: pr * 129 + 129], AF.Copy),
                                r=[("kv_acc",), ("KSD",)], w=[("KSD",)])
                        p.op("gp", lambda e_: e_.iota(selt[:], [[1, 128]],
                                                      channel_multiplier=-64,
                                                      allow_small_or_imprecise_dtypes=True),
                             w=[("selt",)])
                        for t2 in range(8):
                            p.op("dve", lambda v, t2=t2: v.tensor_single_scalar(
                                sel1[:], selt[:], float(-128 * t2), ALU.is_ge),
                                r=[("selt",), ("sel1",)], w=[("sel1",)])
                            p.op("dve", lambda v, t2=t2: v.tensor_single_scalar(
                                sel2[:], selt[:], float(64 - 128 * t2), ALU.is_lt),
                                r=[("selt",), ("sel2",)], w=[("sel2",)])
                            p.op("dve", lambda v, t2=t2: v.tensor_tensor(
                                SEL[:, t2 * 128:(t2 + 1) * 128], sel1[:], sel2[:], ALU.mult),
                                r=[("sel1",), ("sel2",)], w=[("SEL",)])
                        p.barrier()

                        for ch in range(NCH):
                            def mm_z(t_, ch=ch):
                                last = None
                                for k in range(8):
                                    last = t_.matmul(
                                        zps[0:16, :], KSD[:, k * 16:(k + 1) * 16],
                                        q_full[:, k * T + ch * CH: k * T + (ch + 1) * CH],
                                        start=(k == 0), stop=(k == 7))
                                return last
                            p.op("pe", mm_z,
                                 r=[("KSD",)] + [("q", ch, fm) for fm in range(8)],
                                 w=[("zps",)])
                            p.op("dve", lambda v: v.tensor_scalar(
                                rz_sb[:], zps[0:16, :], AT_EPS, None, ALU.add),
                                r=[("zps",), ("rzf",)], w=[("rzf",)])
                            p.op("dve", lambda v: v.reciprocal(rz_sb[:], rz_sb[:]),
                                 r=[("rzf",)], w=[("rzf",)])
                            p.op("dve", lambda v: v.tensor_copy(rzb[:], rz_sb[:]),
                                 r=[("rzf",), ("rzb",)], w=[("rzb",)])
                            for fm in range(8):
                                p.op("pe", lambda t_, fm=fm: t_.matmul(
                                    rzps[:], SEL[:, fm * 128:(fm + 1) * 128], rzb[:],
                                    start=True, stop=True),
                                    r=[("SEL",), ("rzb",)], w=[("rzps",)])
                                p.op("act", lambda a: a.activation(rzx[:], rzps[:], AF.Copy),
                                     r=[("rzps",), ("rzx",)], w=[("rzx",)])
                                p.op("dve", lambda v, fm=fm, ch=ch: v.tensor_tensor(
                                    qs_sb[:, fm * 512:(fm + 1) * 512],
                                    q_full[:, fm * T + ch * CH: fm * T + (ch + 1) * CH],
                                    rzx[:], ALU.mult),
                                    r=[("rzx",), ("q", ch, fm), ("qs", fm)], w=[("qs", fm)])
                                ap_ = atps[fm % 2]
                                p.op("pe", lambda t_, fm=fm, ap_=ap_: t_.matmul(
                                    ap_[:], kvD[:, fm * 128:(fm + 1) * 128],
                                    qs_sb[:, fm * 512:(fm + 1) * 512],
                                    start=True, stop=True),
                                    r=[("kvD", fm), ("qs", fm)], w=[("atp", fm % 2)])
                                p.op("act", lambda a, fm=fm, ap_=ap_: a.activation(
                                    attS[:, fm * 512:(fm + 1) * 512], ap_[:], AF.Copy),
                                    r=[("atp", fm % 2), ("attS", fm)], w=[("attS", fm)])
                            for dm in range(8):
                                yp = yaps[dm % 2]

                                def mm_wo(t_, dm=dm, yp=yp):
                                    last = None
                                    for k in range(8):
                                        last = t_.matmul(
                                            yp[:],
                                            wob[:, k * D + dm * 128: k * D + dm * 128 + 128],
                                            attS[:, k * 512:(k + 1) * 512],
                                            start=(k == 0), stop=(k == 7))
                                    return last
                                p.op("pe", mm_wo,
                                     r=[("attS", k) for k in range(8)] + [("wo",)],
                                     w=[("yap", dm % 2)])
                                p.op("dve", lambda v, dm=dm, yp=yp: v.tensor_scalar(
                                    ya_sb[:, dm * 512:(dm + 1) * 512], yp[:],
                                    bo_sb[:, dm:dm + 1], None, ALU.add),
                                    r=[("yap", dm % 2), ("bo",), ("ya", dm)], w=[("ya", dm)])
                            for j in range(4):
                                tt = ch * 4 + j
                                o2 = o2_sb[j]
                                p.op("sp", lambda e_, tt=tt, o2=o2: e_.dma_start(
                                    out=o2[:], in_=out_h[tt * 128:(tt + 1) * 128, :]),
                                    r=[("out_h", tt), ("bo2", j)], w=[("bo2", j)], dma=True)
                                for dm in range(8):
                                    fp = ftp[dm // 4]
                                    p.op("pe", lambda t_, dm=dm, j=j, fp=fp: t_.transpose(
                                        fp[:, (dm % 4) * 128:(dm % 4) * 128 + 128],
                                        ya_sb[:, dm * 512 + j * 128: dm * 512 + j * 128 + 128],
                                        identf[:]),
                                        r=[("ya", dm), ("identf",)], w=[("ftp", dm // 4)])
                                oF = oF_sb[j % 2]
                                for half in range(2):
                                    p.op("dve", lambda v, half=half, oF=oF, o2=o2:
                                         v.tensor_tensor(
                                             oF[:, half * 512:(half + 1) * 512],
                                             ftp[half][:],
                                             o2[:, half * 512:(half + 1) * 512], ALU.add),
                                         r=[("ftp", half), ("bo2", j), ("boF", j % 2)],
                                         w=[("boF", j % 2)])
                                p.op("sp", lambda e_, tt=tt, oF=oF: e_.dma_start(
                                    out=out_h[tt * 128:(tt + 1) * 128, :], in_=oF[:]),
                                    r=[("boF", j % 2)], w=[("out_h", tt)], dma=True)
                        if debug:
                            p.op("sp", lambda e_: e_.dma_start(out=dbg["kv"], in_=kv_acc[:]),
                                 r=[("kv_acc",)], w=[], dma=True)
                            p.op("sp", lambda e_: e_.dma_start(out=dbg["q"], in_=q_full[:]),
                                 r=[("q", ch2, fm2) for ch2 in range(NCH) for fm2 in range(8)],
                                 w=[], dma=True)
                        p.barrier()

        p.barrier()
        p.finalize()

        from contextlib import ExitStack as ES2
        with ES2() as sctx:
            sems = {k: sctx.enter_context(nc.semaphore(f"sem_{k}"))
                    for k in sorted(p.sem_keys | set(ENGINES))}
            per_eng = {en: [] for en in ENGINES}
            for i, o in enumerate(p.ops):
                per_eng[o["eng"]].append(o)
            with nc.Block() as block:
                def body_for(en):
                    def body(eng):
                        for o in per_eng[en]:
                            for d in o["waits"]:
                                eng.wait_ge(sems[p.ops[d]["sem"]], p.target[d])
                            rw = o.get("relay_wait")
                            if rw is not None:
                                inst = eng.wait_ge(sems[en], rw)
                                if o["needs_inc"]:
                                    inst.then_inc(sems[o["sem"]], o["inc_amt"])
                            elif o["fn"] is not None:
                                inst = o["fn"](eng)
                                if o["needs_inc"]:
                                    inst.then_inc(sems[o["sem"]], o["inc_amt"])
                    return body
                block.tensor(body_for("pe"))
                block.vector(body_for("dve"))
                block.scalar(body_for("act"))
                block.sync(body_for("sp"))
                block.gpsimd(body_for("gp"))
    return nc


def kernel(L0_attn_x, L0_attn_Wq, L0_attn_Wk, L0_attn_Wv, L0_attn_Wo, L0_attn_bo,
           L0_ffn_w_gate, L0_ffn_w_up, L0_ffn_w_down):
    x = np.ascontiguousarray(np.asarray(L0_attn_x, dtype=np.float32))
    B = x.shape[0]
    nc = build_nc(debug=DEBUG_OUT)
    wg = np.asarray(L0_ffn_w_gate, np.float32)
    wu = np.asarray(L0_ffn_w_up, np.float32)
    wd = np.asarray(L0_ffn_w_down, np.float32)
    base = {
        "wq": np.ascontiguousarray(np.asarray(L0_attn_Wq, np.float32)),
        "wk": np.ascontiguousarray(np.asarray(L0_attn_Wk, np.float32)),
        "wv": np.ascontiguousarray(np.asarray(L0_attn_Wv, np.float32)),
        "wo": np.ascontiguousarray(np.asarray(L0_attn_Wo, np.float32)),
        "bo": np.ascontiguousarray(np.asarray(L0_attn_bo, np.float32)),
    }
    SH, SD = IH // 8, D // 8
    in_maps = []
    for c in range(B):
        m = dict(base, x=np.ascontiguousarray(x[c]))
        for hf in range(2):
            m[f"wgs{hf}"] = np.ascontiguousarray(wg[hf * IH + c * SH: hf * IH + (c + 1) * SH, :])
            m[f"wus{hf}"] = np.ascontiguousarray(wu[hf * IH + c * SH: hf * IH + (c + 1) * SH, :])
            m[f"wds{hf}"] = np.ascontiguousarray(wd[c * SD:(c + 1) * SD, hf * IH:(hf + 1) * IH])
        in_maps.append(m)
    res = run_bass_kernel_spmd(nc, in_maps, core_ids=list(range(B)))
    out = np.stack([np.asarray(res.results[c]["out"], dtype=np.float32) for c in range(B)])
    return out


if __name__ == "__main__":
    nc = build_nc()
    n = len(nc.m.functions[0].blocks) if hasattr(nc, "m") else -1
    print("built ok")


# version 27
# speedup vs baseline: 1.1940x; 1.1940x over previous
import numpy as np

import concourse.bass as bass
import concourse.mybir as mybir
from concourse.bass_utils import run_bass_kernel_spmd

F32 = mybir.dt.float32
BF16 = mybir.dt.bfloat16
AF = mybir.ActivationFunctionType
ALU = mybir.AluOpType
AX = mybir.AxisListType

RC1, RC2, RC3 = 6291456.0, 12582912.0, 25165824.0

T = 4096
D = 1024
I2 = 2816
IH = 1408
CH = 512
NCH = T // CH
NTL = T // 128
LN_EPS = 1e-5
AT_EPS = 1e-6

PHASES = ("a0", "a1", "a2", "b1", "b2")
PARANOID = True
DEBUG_OUT = False

ENGINES = ("pe", "dve", "act", "sp", "gp")


class Planner:

    def __init__(self):
        self.ops = []
        self.last_write = {}
        self.readers = {}
        self.sem_keys = set()
        self.know = {}
        self.snap = {}
        self.dma_count = {}
        self.last_dma = {}
        self.relay_cover = {}
        self.relay_last = {}

    def _K(self, eng):
        return self.know.setdefault(eng, {})

    def _ensure_relay(self, q, need_ord):
        if self.relay_cover.get(q, -1) >= need_ord:
            return self.relay_last[q]
        idx = len(self.ops)
        val = 16 * self.dma_count.get(q, 0)
        self.ops.append(dict(eng=q, sem=q + "r", fn=None, relay_wait=val,
                             waits=[], needs_inc=True, inc_amt=1, tail=False,
                             dma=False))
        self.sem_keys.add(q + "r")
        self.snap[idx] = dict(self._K(q))
        self.relay_cover[q] = self.dma_count.get(q, 0)
        self.relay_last[q] = idx
        return idx

    def _edge(self, eng, K, d):
        o = self.ops[d]
        if o["dma"]:
            q = o["eng"]
            if eng == q:
                d = self.last_dma[q]
            else:
                d = self._ensure_relay(q, o["dma_ord"])
            o = self.ops[d]
        de = o["eng"]
        key = o["sem"]
        if de == eng and o["sem"] == de and not PARANOID and not o["tail"]:
            return None
        if K.get(key, -1) >= d:
            return None
        return d

    def _wait_on(self, eng, K, d, waits):
        o = self.ops[d]
        o["needs_inc"] = True
        waits.append(d)
        for f, v in self.snap.get(d, {}).items():
            if K.get(f, -1) < v:
                K[f] = v
        K[o["sem"]] = max(K.get(o["sem"], -1), d)

    def op(self, eng, fn, r=(), w=(), dma=False, tail=False, coll=False):
        deps = set()
        for t in r:
            d = self.last_write.get(t)
            if d is not None:
                deps.add(d)
        for t in w:
            d = self.last_write.get(t)
            if d is not None:
                deps.add(d)
            for d2 in self.readers.get(t, ()):
                deps.add(d2)
        K = self._K(eng)
        waits = []
        for d in sorted(deps):
            d2 = self._edge(eng, K, d)
            if d2 is not None:
                self._wait_on(eng, K, d2, waits)
        if coll:
            sem = eng + "c"
        elif dma:
            sem = eng
        else:
            sem = eng if eng in ("pe", "dve", "act") else eng + "e"
        idx = len(self.ops)
        self.ops.append(dict(eng=eng, sem=sem, fn=fn, waits=waits,
                             needs_inc=dma or coll, inc_amt=16 if dma else 1,
                             tail=tail, dma=dma))
        self.sem_keys.add(sem)
        self.snap[idx] = dict(K)
        if dma:
            self.dma_count[eng] = self.dma_count.get(eng, 0) + 1
            self.ops[idx]["dma_ord"] = self.dma_count[eng]
            self.last_dma[eng] = idx
        for t in w:
            self.last_write[t] = idx
            self.readers[t] = []
        for t in r:
            self.readers.setdefault(t, []).append(idx)
        return idx

    def barrier(self):
        latest = {}
        for i, o in enumerate(self.ops):
            if o["fn"] is not None or o.get("relay_wait") is not None:
                latest[o["eng"]] = i
        for eng in ENGINES:
            K = self._K(eng)
            waits = []
            for f, d in latest.items():
                if f == eng:
                    continue
                d2 = self._edge(eng, K, d)
                if d2 is not None:
                    self._wait_on(eng, K, d2, waits)
            if waits:
                idx = len(self.ops)
                self.ops.append(dict(eng=eng, sem=eng, fn=None, waits=waits,
                                     needs_inc=False, inc_amt=1, tail=False,
                                     dma=False))
                self.snap[idx] = dict(K)

    def finalize(self):
        cum = {}
        self.target = {}
        for i, o in enumerate(self.ops):
            if o["needs_inc"]:
                cum[o["sem"]] = cum.get(o["sem"], 0) + o["inc_amt"]
                self.target[i] = cum[o["sem"]]


def _quant_plan(p, src_ap, out_ap, nblk, tm, src_toks, out_tok, pre_ap=None,
                pre_toks=(), npart=128, amax_eng="dve", dq_eng="dve"):
    n16 = nblk * 16
    A, B, C = tm["A"], tm["B"], tm["C"]
    E, F, G = tm.get("E"), tm.get("F"), tm["G"]
    if tm.get("swap_ef"):
        E, F = F, E
    lean = out_ap is not None
    am, scl, rs, rr = tm["am"], tm["scl"], tm["rs"], tm["rr"]
    src3 = src_ap.rearrange("p (s n) -> p s n", n=16)
    pfx = tm.get("pfx", "q")
    tA, tB, tC, tE, tF, tG, tS = ((pfx + "A",), (pfx + "B",), (pfx + "C",),
                                  (pfx + "E",), (pfx + "F",), (pfx + "G",),
                                  (pfx + "sm",))
    if tm.get("swap_ef"):
        tE, tF = tF, tE
    if out_ap is None:
        out_ap = E[0:npart, 0:n16]
        out_tok = tE
    out3 = out_ap.rearrange("p (s n) -> p s n", n=16)

    p.op(amax_eng, lambda v: v.tensor_reduce(am[0:npart, 0:nblk].unsqueeze(2), src3,
                                             AX.X, ALU.max, apply_absolute_value=True),
         r=list(src_toks) + [tS], w=[tS], tail=True)
    if pre_ap is not None:
        p.op("dve", lambda v: v.tensor_scalar(am[0:npart, 0:nblk], am[0:npart, 0:nblk], pre_ap,
                                              None, ALU.mult),
             r=[tS] + list(pre_toks), w=[tS])
    p.op("dve", lambda v: v.tensor_scalar(scl[0:npart, 0:nblk], am[0:npart, 0:nblk], 1.0 / 6.0, 1e-12,
                                          ALU.mult, ALU.max), r=[tS], w=[tS])
    p.op("dve", lambda v: v.reciprocal(rs[0:npart, 0:nblk], scl[0:npart, 0:nblk]), r=[tS], w=[tS])
    if pre_ap is not None:
        p.op("dve", lambda v: v.tensor_scalar(rr[0:npart, 0:nblk], rs[0:npart, 0:nblk], pre_ap,
                                              None, ALU.mult),
             r=[tS] + list(pre_toks), w=[tS])
        rsv = rr
    else:
        rsv = rs
    p.op("dve", lambda v: v.tensor_tensor(
        A[0:npart, 0:n16].rearrange("p (s n) -> p s n", n=16), src3,
        rsv[0:npart, 0:nblk].unsqueeze(2).broadcast_to([npart, nblk, 16]), ALU.mult),
        r=list(src_toks) + [tS, tA], w=[tA])
    p.op("dve", lambda v: v.tensor_scalar(B[0:npart, 0:n16], A[0:npart, 0:n16], -2.0, 2.0,
                                          ALU.max, ALU.min), r=[tA, tB], w=[tB])
    p.op("dve", lambda v: v.tensor_scalar(C[0:npart, 0:n16], A[0:npart, 0:n16], -4.0, 4.0,
                                          ALU.max, ALU.min), r=[tA, tC], w=[tC])
    p.op("dve", lambda v: v.tensor_scalar(G[0:npart, 0:n16], B[0:npart, 0:n16], RC1, RC1,
                                          ALU.add, ALU.subtract), r=[tB, tG], w=[tG])
    p.op("dve", lambda v: v.scalar_tensor_tensor(B[0:npart, 0:n16], C[0:npart, 0:n16], RC2,
                                                 B[0:npart, 0:n16], ALU.add, ALU.subtract),
         r=[tB, tC], w=[tB])
    p.op("dve", lambda v: v.scalar_tensor_tensor(A[0:npart, 0:n16], A[0:npart, 0:n16], RC3,
                                                 C[0:npart, 0:n16], ALU.add, ALU.subtract),
         r=[tA, tC], w=[tA])
    if lean:
        p.op("dve", lambda v: v.scalar_tensor_tensor(B[0:npart, 0:n16], B[0:npart, 0:n16],
                                                     -RC2, G[0:npart, 0:n16],
                                                     ALU.add, ALU.add), r=[tB, tG], w=[tB])
        p.op("dve", lambda v: v.scalar_tensor_tensor(B[0:npart, 0:n16], A[0:npart, 0:n16],
                                                     -RC3, B[0:npart, 0:n16],
                                                     ALU.add, ALU.add), r=[tA, tB], w=[tB])
        last, ltok = B, tB
    else:
        p.op("dve", lambda v: v.scalar_tensor_tensor(E[0:npart, 0:n16], B[0:npart, 0:n16],
                                                     -RC2, G[0:npart, 0:n16],
                                                     ALU.add, ALU.add), r=[tB, tG, tE], w=[tE])
        p.op("dve", lambda v: v.scalar_tensor_tensor(E[0:npart, 0:n16], A[0:npart, 0:n16],
                                                     -RC3, E[0:npart, 0:n16],
                                                     ALU.add, ALU.add), r=[tA, tE], w=[tE])
        last, ltok = E, tE
    p.op(dq_eng, lambda v: v.tensor_tensor(
        out3, last[0:npart, 0:n16].rearrange("p (s n) -> p s n", n=16),
        scl[0:npart, 0:nblk].unsqueeze(2).broadcast_to([npart, nblk, 16]), ALU.mult),
        r=[ltok, tS, out_tok], w=[out_tok])


def build_nc(debug=False):
    nc = bass.Bass("TRN2", target_bir_lowering=False)
    x_h = nc.dram_tensor("x", [T, D], F32, kind="ExternalInput").ap()
    wq_h = nc.dram_tensor("wq", [D, D], F32, kind="ExternalInput").ap()
    wk_h = nc.dram_tensor("wk", [D, D], F32, kind="ExternalInput").ap()
    wv_h = nc.dram_tensor("wv", [D, D], F32, kind="ExternalInput").ap()
    wo_h = nc.dram_tensor("wo", [D, D], F32, kind="ExternalInput").ap()
    bo_h = nc.dram_tensor("bo", [D], F32, kind="ExternalInput").ap()
    wgs = [nc.dram_tensor(f"wgs{h}", [IH // 8, D], F32, kind="ExternalInput").ap()
           for h in range(2)]
    wus = [nc.dram_tensor(f"wus{h}", [IH // 8, D], F32, kind="ExternalInput").ap()
           for h in range(2)]
    wds = [nc.dram_tensor(f"wds{h}", [D // 8, IH], F32, kind="ExternalInput").ap()
           for h in range(2)]
    wall_ci = [nc.dram_tensor(f"wall_ci{h}", [528, D], BF16).ap() for h in range(2)]
    wall_co = [nc.dram_tensor(f"wall_co{h}", [8 * 528, D], BF16, addr_space="Shared").ap()
               for h in range(2)]
    out_h = nc.dram_tensor("out", [T, D], F32, kind="ExternalOutput").ap()
    dbg = {}
    if debug:
        dbg["kv"] = nc.dram_tensor("dbg_kv", [128, 8 * 129], F32, kind="ExternalOutput").ap()
        dbg["q"] = nc.dram_tensor("dbg_q", [128, 8 * T], BF16, kind="ExternalOutput").ap()
        dbg["xqT"] = nc.dram_tensor("dbg_xqT", [128, 8 * T], BF16, kind="ExternalOutput").ap()

    from contextlib import ExitStack
    p = Planner()
    with ExitStack() as gctx:
        g = gctx.enter_context
        iot = g(nc.sbuf_tensor("iot", [128, 128], F32))
        identf = g(nc.sbuf_tensor("identf", [128, 128], F32))
        identb = g(nc.sbuf_tensor("identb", [128, 128], BF16))
        m_all = g(nc.sbuf_tensor("m_all", [128, NTL], F32))
        negm = g(nc.sbuf_tensor("negm", [128, NTL], F32))
        rstd = g(nc.sbuf_tensor("rstd", [128, NTL], F32))
        vtmp = g(nc.sbuf_tensor("vtmp", [128, NTL], F32))

        p.op("gp", lambda e: e.iota(iot[:], [[1, 128]], channel_multiplier=-1,
                                    allow_small_or_imprecise_dtypes=True), w=[("iot",)])
        p.op("dve", lambda v: v.tensor_single_scalar(identf[:], iot[:], 0.0, ALU.is_equal),
             r=[("iot",)], w=[("identf",)])
        p.op("dve", lambda v: v.tensor_copy(identb[:], identf[:]),
             r=[("identf",)], w=[("identb",)])

        qs_ctx = ExitStack()
        gctx.enter_context(qs_ctx)
        eq = qs_ctx.enter_context
        qA = eq(nc.sbuf_tensor("qA", [128, IH], F32))
        qB = eq(nc.sbuf_tensor("qB", [128, IH], F32))
        qC = eq(nc.sbuf_tensor("qC", [128, IH], F32))
        qG = eq(nc.sbuf_tensor("qG", [128, IH], BF16))
        qam = eq(nc.sbuf_tensor("qam", [128, IH // 16], F32))
        qscl = eq(nc.sbuf_tensor("qscl", [128, IH // 16], F32))
        qrs = eq(nc.sbuf_tensor("qrs", [128, IH // 16], F32))
        qrr = eq(nc.sbuf_tensor("qrr", [128, IH // 16], F32))
        qtm = dict(A=qA, B=qB, C=qC, G=qG,
                   am=qam, scl=qscl, rs=qrs, rr=qrr)
        qxA = eq(nc.sbuf_tensor("qxA", [128, D], F32))
        qxB = eq(nc.sbuf_tensor("qxB", [128, D], F32))
        qxC = eq(nc.sbuf_tensor("qxC", [128, D], F32))
        qxE = eq(nc.sbuf_tensor("qxE", [128, D], BF16))
        qxF = eq(nc.sbuf_tensor("qxF", [128, D], BF16))
        qxG = eq(nc.sbuf_tensor("qxG", [128, D], BF16))
        qxam = eq(nc.sbuf_tensor("qxam", [128, D // 16], F32))
        qxscl = eq(nc.sbuf_tensor("qxscl", [128, D // 16], F32))
        qxrs = eq(nc.sbuf_tensor("qxrs", [128, D // 16], F32))
        qxrr = eq(nc.sbuf_tensor("qxrr", [128, D // 16], F32))
        qtm2 = dict(A=qxA, B=qxB, C=qxC, E=qxE, F=qxF, G=qxG,
                    am=qxam, scl=qxscl, rs=qxrs, rr=qxrr, pfx="x")
        ssum = eq(nc.sbuf_tensor("ssum", [128, 1], F32))
        ssq = eq(nc.sbuf_tensor("ssq", [128, 1], F32))
        nrt1 = eq(nc.sbuf_tensor("nrt1", [128, 1], F32))
        xqT = eq(nc.sbuf_tensor("xqT", [128, 8 * T], BF16))

        if "a1" in PHASES or "a2" in PHASES:
            with ExitStack() as s:
                e = s.enter_context
                wshp = [e(nc.sbuf_tensor(f"wsh{i}", [128, IH], F32)) for i in range(2)]
                wshqp = [e(nc.sbuf_tensor(f"wshq{i}", [128, IH], BF16)) for i in range(2)]
                pc = 0
                for hf in range(2):
                    jobs = [(wgs[hf], 0, [(0, 128, D), (128, 48, D)]),
                            (wus[hf], 176, [(0, 128, D), (128, 48, D)]),
                            (wds[hf], 352, [(0, 128, IH)])]
                    for src_h, base, pieces in jobs:
                        for (r0, npp, ncols) in pieces:
                            wsh, wshq = wshp[pc % 2], wshqp[pc % 2]
                            wt, wqt_ = ("wsh", pc % 2), ("wshq", pc % 2)
                            pc += 1
                            p.op("sp", lambda e_, src_h=src_h, r0=r0, npp=npp, ncols=ncols,
                                 wsh=wsh: e_.dma_start(out=wsh[0:npp, 0:ncols],
                                                       in_=src_h[r0:r0 + npp, :]),
                                 r=[wt], w=[wt], dma=True)
                            _quant_plan(p, wsh[0:npp, 0:ncols], wshq[0:npp, 0:ncols],
                                        ncols // 16, qtm, [wt], wqt_,
                                        npart=npp)
                            flat = wall_ci[hf].flatten()
                            o0 = (base + r0) * D
                            dst = flat[o0: o0 + npp * ncols].rearrange(
                                "(p n) -> p n", n=ncols)
                            p.op("sp", lambda e_, dst=dst, npp=npp, ncols=ncols,
                                 wshq=wshq: e_.dma_start(out=dst,
                                                         in_=wshq[0:npp, 0:ncols]),
                                 r=[wqt_], w=[("wci", hf), wqt_], dma=True)
                    p.op("gp", lambda g_, hf=hf: g_.collective_compute(
                        "AllGather", ALU.bypass,
                        replica_groups=[list(range(8))],
                        ins=[wall_ci[hf].opt()], outs=[wall_co[hf].opt()]),
                        r=[("wci", hf), ("ccq",)],
                        w=[("wgath", hf), ("ccq",)], coll=True)

        if "a0" in PHASES:
            with ExitStack() as s:
                e = s.enter_context
                x_sb = [e(nc.sbuf_tensor(f"a0x{i}", [128, D], F32)) for i in range(3)]
                sq = e(nc.sbuf_tensor("a0sq", [128, D], F32))
                ssum = e(nc.sbuf_tensor("a0ssum", [128, 1], F32))
                ssq = e(nc.sbuf_tensor("a0ssq", [128, 1], F32))
                for tt in range(NTL):
                    xb = x_sb[tt % 3]
                    xt = ("a0x", tt % 3)
                    p.op("sp", lambda e_, tt=tt, xb=xb: e_.dma_start(
                        out=xb[:], in_=x_h[tt * 128:(tt + 1) * 128, :]),
                        w=[xt], dma=True)
                    p.op("dve", lambda v, xb=xb: v.reduce_sum(ssum[:], xb[:], AX.X),
                         r=[xt], w=[("a0s",)], tail=True)
                    p.op("dve", lambda v, tt=tt: v.tensor_scalar(
                        m_all[:, tt:tt + 1], ssum[:], 1.0 / D, None, ALU.mult),
                        r=[("a0s",)], w=[("m", tt)])
                    p.op("act", lambda a, xb=xb: a.activation(sq[:], xb[:], AF.Square,
                                                              accum_out=ssq[:]),
                         r=[xt], w=[("a0sq",), ("a0ssq",)], tail=True)
                    p.op("dve", lambda v, tt=tt: v.tensor_tensor(
                        vtmp[:, tt:tt + 1], m_all[:, tt:tt + 1], m_all[:, tt:tt + 1],
                        ALU.mult), r=[("m", tt)], w=[("vt", tt)])
                    p.op("dve", lambda v, tt=tt: v.tensor_scalar(
                        vtmp[:, tt:tt + 1], vtmp[:, tt:tt + 1], LN_EPS, None,
                        ALU.subtract), r=[("vt", tt)], w=[("vt", tt)])
                    p.op("dve", lambda v, tt=tt: v.scalar_tensor_tensor(
                        vtmp[:, tt:tt + 1], ssq[:], 1.0 / D, vtmp[:, tt:tt + 1],
                        ALU.mult, ALU.subtract),
                        r=[("a0ssq",), ("vt", tt)], w=[("vt", tt)])
                    p.op("act", lambda a, tt=tt: a.activation(
                        vtmp[:, tt:tt + 1], vtmp[:, tt:tt + 1], AF.Sqrt),
                        r=[("vt", tt)], w=[("vt", tt)])
                    p.op("dve", lambda v, tt=tt: v.reciprocal(
                        rstd[:, tt:tt + 1], vtmp[:, tt:tt + 1]),
                        r=[("vt", tt)], w=[("rstd", tt)])
                    p.op("dve", lambda v, tt=tt: v.tensor_scalar(
                        negm[:, tt:tt + 1], m_all[:, tt:tt + 1], -1.0, None, ALU.mult),
                        r=[("m", tt)], w=[("negm", tt)])
                p.barrier()

        if True:
            def plan_ffn_half(hf):
                with ExitStack() as s:
                    e = s.enter_context
                    WgT = e(nc.sbuf_tensor(f"WgT{hf}", [128, 8 * IH], BF16))
                    WuT = e(nc.sbuf_tensor(f"WuT{hf}", [128, 8 * IH], BF16))
                    WdT = e(nc.sbuf_tensor(f"WdT{hf}", [128, 11 * D], BF16))
                    wqt = [e(nc.sbuf_tensor(f"wqt{hf}_{i}", [128, IH], BF16))
                           for i in range(2)]
                    trp = [e(nc.psum_tensor(f"wtr{hf}_{i}", [128, 1024], BF16))
                           for i in range(2)]

                    wco_flat = wall_co[hf].flatten()

                    def load_w(base, dstT, ntiles, ncols, n_tr, off_fn, dtok):
                        per = 176 if ncols == D else 128
                        for r in range(ntiles):
                            wq_ = wqt[r % 2]
                            wtok = ("wqt", hf, r % 2)
                            p0 = 0
                            while p0 < 128:
                                g = r * 128 + p0
                                rank, loc = g // per, g % per
                                n = min(128 - p0, per - loc)
                                if ncols == D:
                                    o0 = (rank * 528 + base + loc) * D
                                else:
                                    o0 = (rank * 528 + base) * D + loc * ncols
                                ap_in = wco_flat[o0: o0 + n * ncols].rearrange(
                                    "(p n) -> p n", n=ncols)
                                p.op("sp", lambda e_, ap_in=ap_in, p0=p0, n=n,
                                     ncols=ncols, wq_=wq_:
                                     e_.dma_start(out=wq_[p0:p0 + n, 0:ncols], in_=ap_in),
                                     r=[wtok, ("wgath", hf)], w=[wtok], dma=True)
                                p0 += n
                            for k in range(n_tr):
                                ps = trp[k % 2]
                                pstok = ("trp", hf, k % 2)
                                p.op("pe", lambda t_, k=k, ps=ps, wq_=wq_: t_.transpose(
                                    ps[:, (k % 8) * 128:(k % 8) * 128 + 128],
                                    wq_[:, k * 128:(k + 1) * 128], identb[:]),
                                    r=[wtok, ("identb",)], w=[pstok])
                                off = off_fn(r, k)
                                p.op("act", lambda a, ps=ps, k=k, off=off, dstT=dstT:
                                     a.activation(dstT[:, off:off + 128],
                                                  ps[:, (k % 8) * 128:(k % 8) * 128 + 128],
                                                  AF.Copy),
                                     r=[pstok], w=[dtok])

                    def load_w_all():
                        load_w(0, WgT, 11, D, 8,
                               lambda r, k: k * IH + r * 128, ("WgT", hf))
                        load_w(176, WuT, 11, D, 8,
                               lambda r, k: k * IH + r * 128, ("WuT", hf))
                        load_w(352, WdT, 8, IH, 11,
                               lambda r, k: k * D + r * 128, ("WdT", hf))

                    res_sb = [e(nc.sbuf_tensor(f"ars{hf}_{i}", [128, D], F32)) for i in range(2)]
                    sg_sb = e(nc.sbuf_tensor(f"asg{hf}", [128, 512], F32))
                    hqT_sb = [e(nc.sbuf_tensor(f"ahqT{hf}_{i}", [128, 11 * 128], BF16))
                              for i in range(2)]
                    hqo = [e(nc.sbuf_tensor(f"ahqo{hf}_{i}", [128, IH], BF16))
                           for i in range(2)]
                    outF = [e(nc.sbuf_tensor(f"aoF{hf}_{i}", [128, D], F32)) for i in range(2)]
                    gps = [e(nc.psum_tensor(f"agps{hf}_{i}", [128, 512], F32)) for i in range(2)]
                    ups = [e(nc.psum_tensor(f"aups{hf}_{i}", [128, 512], F32)) for i in range(2)]
                    dnps = [e(nc.psum_tensor(f"adn{hf}_{i}", [128, 512], F32)) for i in range(2)]

                    def stage1(tt):
                        p.op("sp", lambda e_, tt=tt: e_.dma_start(
                            out=qxA[:, 0:D], in_=x_h[tt * 128:(tt + 1) * 128, :]),
                            r=[("xA",)], w=[("xA",)], dma=True)
                        p.op("act", lambda a, tt=tt: a.activation(
                            qxA[:, 0:D], qxA[:, 0:D], AF.Identity, bias=negm[:, tt:tt + 1]),
                            r=[("xA",), ("negm", tt)], w=[("xA",)])
                        _quant_plan(p, qxA[:, 0:D], None, D // 16,
                                    dict(qtm2, swap_ef=(tt % 2 == 1)),
                                    [("xA",)], None,
                                    pre_ap=rstd[:, tt:tt + 1], pre_toks=[("rstd", tt)])

                    def stage2(tt):
                        xq = qxE if tt % 2 == 0 else qxF
                        xtok = ("xE",) if tt % 2 == 0 else ("xF",)
                        ch, j = tt // 4, tt % 4
                        for k in range(8):
                            ps = trp[k % 2]
                            pstok = ("trp", hf, k % 2)
                            p.op("pe", lambda t_, k=k, ps=ps, xq=xq: t_.transpose(
                                ps[:, (k % 8) * 128:(k % 8) * 128 + 128],
                                xq[:, k * 128:(k + 1) * 128], identb[:]),
                                r=[xtok, ("identb",)], w=[pstok])
                            off = k * T + ch * CH + j * 128
                            p.op("act", lambda a, ps=ps, k=k, off=off: a.activation(
                                xqT[:, off:off + 128],
                                ps[:, (k % 8) * 128:(k % 8) * 128 + 128], AF.Copy),
                                r=[pstok], w=[("xqT", tt)])

                    def stage3(tt):
                        rb = res_sb[tt % 2]
                        rtok = ("ars", hf, tt % 2)
                        if hf == 0:
                            p.op("sp", lambda e_, tt=tt, rb=rb: e_.dma_start(
                                out=rb[:], in_=x_h[tt * 128:(tt + 1) * 128, :]),
                                r=[rtok], w=[rtok], dma=True)
                        else:
                            p.op("sp", lambda e_, tt=tt, rb=rb: e_.dma_start(
                                out=rb[:], in_=out_h[tt * 128:(tt + 1) * 128, :]),
                                r=[rtok, ("out_h", tt)], w=[rtok], dma=True)
                        for ic, (i0, iw) in enumerate(((0, 512), (512, 512), (1024, 384))):
                            gp_, up_ = gps[ic % 2], ups[ic % 2]
                            gt, ut = ("gps", hf, ic % 2), ("ups", hf, ic % 2)

                            def mm_gateup(t_, tt=tt, i0=i0, iw=iw, gp_=gp_, up_=up_):
                                last = None
                                for k in range(8):
                                    lhs = xqT[:, k * T + tt * 128: k * T + tt * 128 + 128]
                                    t_.matmul(gp_[:, 0:iw], lhs,
                                              WgT[:, k * IH + i0: k * IH + i0 + iw],
                                              start=(k == 0), stop=(k == 7))
                                    last = t_.matmul(up_[:, 0:iw], lhs,
                                                     WuT[:, k * IH + i0: k * IH + i0 + iw],
                                                     start=(k == 0), stop=(k == 7))
                                return last
                            p.op("pe", mm_gateup,
                                 r=[("xqT", tt), ("WgT", hf), ("WuT", hf)],
                                 w=[gt, ut])
                            p.op("act", lambda a, gp_=gp_, iw=iw: a.activation(
                                sg_sb[:, 0:iw], gp_[:, 0:iw], AF.Silu),
                                r=[gt, ("sg", hf)], w=[("sg", hf)])
                            p.op("dve", lambda v, up_=up_, i0=i0, iw=iw: v.tensor_tensor(
                                qA[:, i0:i0 + iw], sg_sb[:, 0:iw], up_[:, 0:iw],
                                ALU.mult),
                                r=[("sg", hf), ut, ("qA",)], w=[("qA",)])
                        _quant_plan(p, qA[:], hqo[tt % 2][:], IH // 16,
                                    dict(qtm, swap_ef=(tt % 2 == 1)),
                                    [("qA",)], ("hqo", tt % 2))

                    def stage4(tt):
                        hq = hqo[tt % 2]
                        htok = ("hqo", tt % 2)
                        rb = res_sb[tt % 2]
                        rtok = ("ars", hf, tt % 2)
                        hqT = hqT_sb[tt % 2]
                        for k in range(11):
                            ps = trp[k % 2]
                            pstok = ("trp", hf, k % 2)
                            p.op("pe", lambda t_, k=k, ps=ps, hq=hq: t_.transpose(
                                ps[:, (k % 8) * 128:(k % 8) * 128 + 128],
                                hq[:, k * 128:(k + 1) * 128], identb[:]),
                                r=[htok, ("identb",)], w=[pstok])
                            p.op("act", lambda a, ps=ps, k=k, hqT=hqT: a.activation(
                                hqT[:, k * 128:(k + 1) * 128],
                                ps[:, (k % 8) * 128:(k % 8) * 128 + 128], AF.Copy),
                                r=[pstok], w=[("hqT", hf, tt % 2)])
                        oF = outF[tt % 2]
                        for dh2 in range(2):
                            dn = dnps[dh2]
                            dtok = ("dn", hf, dh2)

                            def mm_down(t_, hqT=hqT, dh2=dh2, dn=dn):
                                last = None
                                for k in range(11):
                                    last = t_.matmul(
                                        dn[:], hqT[:, k * 128:(k + 1) * 128],
                                        WdT[:, k * D + dh2 * 512: k * D + dh2 * 512 + 512],
                                        start=(k == 0), stop=(k == 10))
                                return last
                            p.op("pe", mm_down,
                                 r=[("hqT", hf, tt % 2), ("WdT", hf)], w=[dtok])
                            p.op("dve", lambda v, dn=dn, dh2=dh2, oF=oF, rb=rb:
                                 v.tensor_tensor(oF[:, dh2 * 512:(dh2 + 1) * 512], dn[:],
                                                 rb[:, dh2 * 512:(dh2 + 1) * 512], ALU.add),
                                 r=[dtok, rtok, ("oF", hf, tt % 2)],
                                 w=[("oF", hf, tt % 2)])
                        p.op("sp", lambda e_, tt=tt, oF=oF: e_.dma_start(
                            out=out_h[tt * 128:(tt + 1) * 128, :], in_=oF[:]),
                            r=[("oF", hf, tt % 2)], w=[("out_h", tt)], dma=True)

                    if hf == 0:
                        PRO = 4
                        for i in range(NTL + PRO + 2):
                            if i < NTL:
                                stage1(i)
                            if 0 <= i - 1 < NTL:
                                stage2(i - 1)
                            if i == PRO - 1:
                                load_w_all()
                            if 0 <= i - PRO < NTL:
                                stage3(i - PRO)
                            if 0 <= i - PRO - 1 < NTL:
                                stage4(i - PRO - 1)
                    else:
                        load_w_all()
                        for i in range(NTL + 1):
                            if i < NTL:
                                stage3(i)
                            if 0 <= i - 1 < NTL:
                                stage4(i - 1)
                    p.barrier()
                    p.barrier()

            if "a1" in PHASES:
                plan_ffn_half(0)
            if debug:
                p.op("sp", lambda e_: e_.dma_start(out=dbg["xqT"], in_=xqT[:]),
                     r=[("xqT", tt2) for tt2 in range(NTL)], w=[], dma=True)
            if "a2" in PHASES:
                plan_ffn_half(1)
            p.barrier()

        qs_ctx.close()

        if "b1" in PHASES or "b2" in PHASES:
            with ExitStack() as bs:
                eb = bs.enter_context
                q_full = eb(nc.sbuf_tensor("q_full", [128, 8 * T], BF16))
                kv_acc = eb(nc.sbuf_tensor("kv_acc", [128, 8 * 129], F32))
                p.op("dve", lambda v: v.memset(kv_acc[:], 0.0), w=[("kv_acc",)])

                if "b1" in PHASES:
                    with ExitStack() as s:
                        e = s.enter_context
                        wqb = e(nc.sbuf_tensor("wqb", [128, 8 * D], BF16))
                        wkb = e(nc.sbuf_tensor("wkb", [128, 8 * D], BF16))
                        wvb = e(nc.sbuf_tensor("wvb", [128, 8 * D], BF16))
                        x_sb = [e(nc.sbuf_tensor(f"bx{i}", [128, D], F32)) for i in range(2)]
                        xn_sb = [e(nc.sbuf_tensor(f"bxn{i}", [128, D], BF16)) for i in range(2)]
                        xnT = [e(nc.sbuf_tensor(f"bxnT{c}", [128, 8 * CH], BF16))
                               for c in range(2)]
                        k_sb = [e(nc.sbuf_tensor(f"bk{i}", [128, D], BF16)) for i in range(8)]
                        v_sb = [e(nc.sbuf_tensor(f"bv{i}", [128, 8 * 130], BF16))
                                for i in range(8)]
                        m1_sb = e(nc.sbuf_tensor("bm1", [128, 512], F32))
                        e1_sb = e(nc.sbuf_tensor("be1", [128, 512], F32))
                        trp = [e(nc.psum_tensor(f"btr{i}", [128, 1024], BF16)) for i in range(2)]
                        qps = [e(nc.psum_tensor(f"bqps{i}", [128, 512], F32)) for i in range(2)]
                        kvp = [e(nc.psum_tensor(f"bkv{i}", [128, 512], F32)) for i in range(2)]

                        for nm, hsrc, dst in (("wq", wq_h, wqb), ("wk", wk_h, wkb),
                                              ("wv", wv_h, wvb)):
                            for k in range(8):
                                p.op("gp", lambda e_, hsrc=hsrc, dst=dst, k=k: e_.dma_start(
                                    out=dst[:, k * D:(k + 1) * D],
                                    in_=hsrc[k * 128:(k + 1) * 128, :]),
                                    w=[(nm,)], dma=True)
                        for j in range(8):
                            p.op("dve", lambda v, j=j: v.memset(
                                v_sb[j][:].rearrange("p (a c) -> p a c", c=130)[:, :, 128:130],
                                1.0), w=[("vones", j)])

                        def b1_tiles(ch):
                            cp = ch % 2
                            xT = xnT[cp]
                            for j in range(4):
                                tt = ch * 4 + j
                                xb = x_sb[j % 2]
                                xtok = ("bx", j % 2)
                                p.op("sp", lambda e_, tt=tt, xb=xb: e_.dma_start(
                                    out=xb[:], in_=x_h[tt * 128:(tt + 1) * 128, :]),
                                    r=[xtok], w=[xtok], dma=True)
                                xn = xn_sb[j % 2]
                                xntok = ("bxn", j % 2)
                                p.op("dve", lambda v, tt=tt, xb=xb, xn=xn:
                                     v.scalar_tensor_tensor(
                                         xn[:], xb[:], m_all[:, tt:tt + 1],
                                         rstd[:, tt:tt + 1].broadcast_to([128, D]),
                                         ALU.subtract, ALU.mult),
                                     r=[xtok, ("m", tt), ("rstd", tt), xntok], w=[xntok])
                                for k in range(8):
                                    ps = trp[k % 2]
                                    pstok = ("btrp", k % 2)
                                    p.op("pe", lambda t_, k=k, ps=ps, xn=xn: t_.transpose(
                                        ps[:, (k % 8) * 128:(k % 8) * 128 + 128],
                                        xn[:, k * 128:(k + 1) * 128], identb[:]),
                                        r=[xntok, ("identb",)], w=[pstok])
                                    p.op("act", lambda a, ps=ps, k=k, j=j, xT=xT: a.activation(
                                        xT[:, k * CH + j * 128: k * CH + j * 128 + 128],
                                        ps[:, (k % 8) * 128:(k % 8) * 128 + 128], AF.Copy),
                                        r=[pstok], w=[("xnT", cp, k, j)])
                                for fh in range(2):
                                    kp = qps[fh]
                                    ktok = ("bqps", fh)
                                    jb = cp * 4 + j

                                    def mm_k(t_, j=j, fh=fh, kp=kp, xT=xT, W=wkb):
                                        last = None
                                        for k in range(8):
                                            last = t_.matmul(
                                                kp[:],
                                                xT[:, k * CH + j * 128: k * CH + j * 128 + 128],
                                                W[:, k * D + fh * 512: k * D + fh * 512 + 512],
                                                start=(k == 0), stop=(k == 7))
                                        return last
                                    p.op("pe", mm_k,
                                         r=[("xnT", cp, k, j) for k in range(8)] + [("wk",)],
                                         w=[ktok])
                                    p.op("dve", lambda v, kp=kp: v.tensor_scalar(
                                        m1_sb[:], kp[:], 0.0, None, ALU.min),
                                        r=[ktok, ("bm1",)], w=[("bm1",)])
                                    p.op("act", lambda a: a.activation(e1_sb[:], m1_sb[:],
                                                                       AF.Exp),
                                         r=[("bm1",), ("be1",)], w=[("be1",)])
                                    p.op("dve", lambda v, kp=kp, jb=jb, fh=fh:
                                         v.scalar_tensor_tensor(
                                             k_sb[jb][:, fh * 512:(fh + 1) * 512], kp[:],
                                             0.0, e1_sb[:], ALU.max, ALU.add),
                                         r=[ktok, ("be1",), ("bk", jb)], w=[("bk", jb)])

                                    def mm_v(t_, j=j, fh=fh, kp=kp, xT=xT, W=wvb):
                                        last = None
                                        for k in range(8):
                                            last = t_.matmul(
                                                kp[:],
                                                xT[:, k * CH + j * 128: k * CH + j * 128 + 128],
                                                W[:, k * D + fh * 512: k * D + fh * 512 + 512],
                                                start=(k == 0), stop=(k == 7))
                                        return last
                                    p.op("pe", mm_v,
                                         r=[("xnT", cp, k, j) for k in range(8)] + [("wv",)],
                                         w=[ktok])
                                    p.op("act", lambda a, kp=kp, jb=jb, fh=fh: a.activation(
                                        v_sb[jb][:].rearrange("p (a c) -> p a c", c=130)
                                        [:, 4 * fh:4 * fh + 4, 0:128],
                                        kp[:].rearrange("p (a c) -> p a c", c=128), AF.Copy),
                                        r=[ktok, ("bv", jb)], w=[("bv", jb)])

                        def b1_qkv(ch):
                            cp = ch % 2
                            xT = xnT[cp]
                            xnT_toks = [("xnT", cp, k, j) for k in range(8) for j in range(4)]
                            for fm in range(8):
                                qp = qps[fm % 2]
                                qtok = ("bqps", fm % 2)

                                def mm_q(t_, fm=fm, qp=qp, xT=xT):
                                    last = None
                                    for k in range(8):
                                        last = t_.matmul(
                                            qp[:], wqb[:, k * D + fm * 128: k * D + fm * 128 + 128],
                                            xT[:, k * CH:(k + 1) * CH],
                                            start=(k == 0), stop=(k == 7))
                                    return last
                                p.op("pe", mm_q, r=xnT_toks + [("wq",)], w=[qtok])
                                p.op("dve", lambda v, qp=qp: v.tensor_scalar(
                                    m1_sb[:], qp[:], 0.0, None, ALU.min),
                                    r=[qtok, ("bm1",)], w=[("bm1",)])
                                p.op("act", lambda a: a.activation(e1_sb[:], m1_sb[:], AF.Exp),
                                     r=[("bm1",), ("be1",)], w=[("be1",)])
                                p.op("dve", lambda v, qp=qp, fm=fm, ch=ch:
                                     v.scalar_tensor_tensor(
                                         q_full[:, fm * T + ch * CH: fm * T + (ch + 1) * CH],
                                         qp[:], 0.0, e1_sb[:], ALU.max, ALU.add),
                                     r=[qtok, ("be1",)], w=[("q", ch, fm)])
                            for pr in range(8):
                                kvb = kvp[pr % 2]
                                kvtok = ("bkvp", pr % 2)

                                def mm_kv(t_, pr=pr, kvb=kvb, cp=cp):
                                    last = None
                                    for j in range(4):
                                        last = t_.matmul(
                                            kvb[:, 0:129],
                                            k_sb[cp * 4 + j][:, pr * 128:(pr + 1) * 128],
                                            v_sb[cp * 4 + j][:, pr * 130: pr * 130 + 129],
                                            start=(j == 0), stop=(j == 3))
                                    return last
                                p.op("pe", mm_kv,
                                     r=[("bk", cp * 4 + j) for j in range(4)]
                                       + [("bv", cp * 4 + j) for j in range(4)]
                                       + [("vones", cp * 4 + j) for j in range(4)],
                                     w=[kvtok])
                                p.op("dve", lambda v, pr=pr, kvb=kvb: v.tensor_tensor(
                                    kv_acc[:, pr * 129:(pr + 1) * 129],
                                    kv_acc[:, pr * 129:(pr + 1) * 129],
                                    kvb[:, 0:129], ALU.add),
                                    r=[kvtok, ("kv_acc",)], w=[("kv_acc",)])

                        for ch in range(NCH + 1):
                            if 0 <= ch - 1 < NCH:
                                b1_qkv(ch - 1)
                            if ch < NCH:
                                b1_tiles(ch)
                        p.barrier()

                if "b2" in PHASES:
                    with ExitStack() as s:
                        e = s.enter_context
                        wob = e(nc.sbuf_tensor("wob", [128, 8 * D], BF16))
                        bo_sb = e(nc.sbuf_tensor("bo_sb", [128, 8], F32))
                        kvD = e(nc.sbuf_tensor("kvD", [128, 8 * 128], BF16))
                        KSD = e(nc.sbuf_tensor("KSD", [128, 8 * 16], BF16))
                        rz_sb = e(nc.sbuf_tensor("rz", [16, 512], F32))
                        rzb = e(nc.sbuf_tensor("rzb", [16, 512], BF16))
                        SEL = e(nc.sbuf_tensor("SEL", [16, 8 * 128], BF16))
                        selt = e(nc.sbuf_tensor("selt", [16, 128], F32))
                        sel1 = e(nc.sbuf_tensor("sel1", [16, 128], F32))
                        sel2 = e(nc.sbuf_tensor("sel2", [16, 128], F32))
                        rzx = e(nc.sbuf_tensor("rzx", [128, 512], BF16))
                        qs_sb = e(nc.sbuf_tensor("qs", [128, 8 * 512], BF16))
                        attS = e(nc.sbuf_tensor("attS", [128, 8 * 512], BF16))
                        ya_sb = e(nc.sbuf_tensor("ya", [128, 8 * 512], F32))
                        o2_sb = [e(nc.sbuf_tensor(f"bo2_{i}", [128, D], F32)) for i in range(4)]
                        oF_sb = [e(nc.sbuf_tensor(f"boF{i}", [128, D], F32)) for i in range(2)]
                        zps = e(nc.psum_tensor("bzps", [128, 512], F32))
                        rzps = e(nc.psum_tensor("brzps", [128, 512], F32))
                        atps = [e(nc.psum_tensor(f"batp{i}", [128, 512], F32)) for i in range(2)]
                        yaps = [e(nc.psum_tensor(f"byap{i}", [128, 512], F32)) for i in range(2)]
                        ftp = [e(nc.psum_tensor(f"bftp{i}", [128, 512], F32)) for i in range(2)]

                        for k in range(8):
                            p.op("gp", lambda e_, k=k: e_.dma_start(
                                out=wob[:, k * D:(k + 1) * D],
                                in_=wo_h[k * 128:(k + 1) * 128, :]),
                                w=[("wo",)], dma=True)
                            p.op("gp", lambda e_, k=k: e_.dma_start(
                                out=bo_sb[:, k:k + 1],
                                in_=bo_h[k * 128:(k + 1) * 128].rearrange("(p o) -> p o", o=1)),
                                w=[("bo",)], dma=True)
                        p.op("dve", lambda v: v.memset(KSD[:], 0.0), w=[("KSD",)])
                        for pr in range(8):
                            p.op("act", lambda a, pr=pr: a.activation(
                                kvD[:, pr * 128:(pr + 1) * 128],
                                kv_acc[:, pr * 129: pr * 129 + 128], AF.Copy),
                                r=[("kv_acc",)], w=[("kvD", pr)])
                            p.op("dve", lambda v, pr=pr: v.memset(
                                kvD[0:64, pr * 128 + 64:(pr + 1) * 128], 0.0),
                                r=[("kvD", pr)], w=[("kvD", pr)])
                            p.op("dve", lambda v, pr=pr: v.memset(
                                kvD[64:128, pr * 128: pr * 128 + 64], 0.0),
                                r=[("kvD", pr)], w=[("kvD", pr)])
                            p.op("act", lambda a, pr=pr: a.activation(
                                KSD[0:64, pr * 16 + 2 * pr: pr * 16 + 2 * pr + 1],
                                kv_acc[0:64, pr * 129 + 128: pr * 129 + 129], AF.Copy),
                                r=[("kv_acc",), ("KSD",)], w=[("KSD",)])
                            p.op("act", lambda a, pr=pr: a.activation(
                                KSD[64:128, pr * 16 + 2 * pr + 1: pr * 16 + 2 * pr + 2],
                                kv_acc[64:128, pr * 129 + 128: pr * 129 + 129], AF.Copy),
                                r=[("kv_acc",), ("KSD",)], w=[("KSD",)])
                        p.op("gp", lambda e_: e_.iota(selt[:], [[1, 128]],
                                                      channel_multiplier=-64,
                                                      allow_small_or_imprecise_dtypes=True),
                             w=[("selt",)])
                        for t2 in range(8):
                            p.op("dve", lambda v, t2=t2: v.tensor_single_scalar(
                                sel1[:], selt[:], float(-128 * t2), ALU.is_ge),
                                r=[("selt",), ("sel1",)], w=[("sel1",)])
                            p.op("dve", lambda v, t2=t2: v.tensor_single_scalar(
                                sel2[:], selt[:], float(64 - 128 * t2), ALU.is_lt),
                                r=[("selt",), ("sel2",)], w=[("sel2",)])
                            p.op("dve", lambda v, t2=t2: v.tensor_tensor(
                                SEL[:, t2 * 128:(t2 + 1) * 128], sel1[:], sel2[:], ALU.mult),
                                r=[("sel1",), ("sel2",)], w=[("SEL",)])
                        p.barrier()

                        for ch in range(NCH):
                            def mm_z(t_, ch=ch):
                                last = None
                                for k in range(8):
                                    last = t_.matmul(
                                        zps[0:16, :], KSD[:, k * 16:(k + 1) * 16],
                                        q_full[:, k * T + ch * CH: k * T + (ch + 1) * CH],
                                        start=(k == 0), stop=(k == 7))
                                return last
                            p.op("pe", mm_z,
                                 r=[("KSD",)] + [("q", ch, fm) for fm in range(8)],
                                 w=[("zps",)])
                            p.op("dve", lambda v: v.tensor_scalar(
                                rz_sb[:], zps[0:16, :], AT_EPS, None, ALU.add),
                                r=[("zps",), ("rzf",)], w=[("rzf",)])
                            p.op("dve", lambda v: v.reciprocal(rz_sb[:], rz_sb[:]),
                                 r=[("rzf",)], w=[("rzf",)])
                            p.op("dve", lambda v: v.tensor_copy(rzb[:], rz_sb[:]),
                                 r=[("rzf",), ("rzb",)], w=[("rzb",)])
                            for fm in range(8):
                                p.op("pe", lambda t_, fm=fm: t_.matmul(
                                    rzps[:], SEL[:, fm * 128:(fm + 1) * 128], rzb[:],
                                    start=True, stop=True),
                                    r=[("SEL",), ("rzb",)], w=[("rzps",)])
                                p.op("act", lambda a: a.activation(rzx[:], rzps[:], AF.Copy),
                                     r=[("rzps",), ("rzx",)], w=[("rzx",)])
                                p.op("dve", lambda v, fm=fm, ch=ch: v.tensor_tensor(
                                    qs_sb[:, fm * 512:(fm + 1) * 512],
                                    q_full[:, fm * T + ch * CH: fm * T + (ch + 1) * CH],
                                    rzx[:], ALU.mult),
                                    r=[("rzx",), ("q", ch, fm), ("qs", fm)], w=[("qs", fm)])
                                ap_ = atps[fm % 2]
                                p.op("pe", lambda t_, fm=fm, ap_=ap_: t_.matmul(
                                    ap_[:], kvD[:, fm * 128:(fm + 1) * 128],
                                    qs_sb[:, fm * 512:(fm + 1) * 512],
                                    start=True, stop=True),
                                    r=[("kvD", fm), ("qs", fm)], w=[("atp", fm % 2)])
                                p.op("act", lambda a, fm=fm, ap_=ap_: a.activation(
                                    attS[:, fm * 512:(fm + 1) * 512], ap_[:], AF.Copy),
                                    r=[("atp", fm % 2), ("attS", fm)], w=[("attS", fm)])
                            for dm in range(8):
                                yp = yaps[dm % 2]

                                def mm_wo(t_, dm=dm, yp=yp):
                                    last = None
                                    for k in range(8):
                                        last = t_.matmul(
                                            yp[:],
                                            wob[:, k * D + dm * 128: k * D + dm * 128 + 128],
                                            attS[:, k * 512:(k + 1) * 512],
                                            start=(k == 0), stop=(k == 7))
                                    return last
                                p.op("pe", mm_wo,
                                     r=[("attS", k) for k in range(8)] + [("wo",)],
                                     w=[("yap", dm % 2)])
                                p.op("dve", lambda v, dm=dm, yp=yp: v.tensor_scalar(
                                    ya_sb[:, dm * 512:(dm + 1) * 512], yp[:],
                                    bo_sb[:, dm:dm + 1], None, ALU.add),
                                    r=[("yap", dm % 2), ("bo",), ("ya", dm)], w=[("ya", dm)])
                            for j in range(4):
                                tt = ch * 4 + j
                                o2 = o2_sb[j]
                                p.op("sp", lambda e_, tt=tt, o2=o2: e_.dma_start(
                                    out=o2[:], in_=out_h[tt * 128:(tt + 1) * 128, :]),
                                    r=[("out_h", tt), ("bo2", j)], w=[("bo2", j)], dma=True)
                                for dm in range(8):
                                    fp = ftp[dm // 4]
                                    p.op("pe", lambda t_, dm=dm, j=j, fp=fp: t_.transpose(
                                        fp[:, (dm % 4) * 128:(dm % 4) * 128 + 128],
                                        ya_sb[:, dm * 512 + j * 128: dm * 512 + j * 128 + 128],
                                        identf[:]),
                                        r=[("ya", dm), ("identf",)], w=[("ftp", dm // 4)])
                                oF = oF_sb[j % 2]
                                for half in range(2):
                                    p.op("dve", lambda v, half=half, oF=oF, o2=o2:
                                         v.tensor_tensor(
                                             oF[:, half * 512:(half + 1) * 512],
                                             ftp[half][:],
                                             o2[:, half * 512:(half + 1) * 512], ALU.add),
                                         r=[("ftp", half), ("bo2", j), ("boF", j % 2)],
                                         w=[("boF", j % 2)])
                                p.op("sp", lambda e_, tt=tt, oF=oF: e_.dma_start(
                                    out=out_h[tt * 128:(tt + 1) * 128, :], in_=oF[:]),
                                    r=[("boF", j % 2)], w=[("out_h", tt)], dma=True)
                        if debug:
                            p.op("sp", lambda e_: e_.dma_start(out=dbg["kv"], in_=kv_acc[:]),
                                 r=[("kv_acc",)], w=[], dma=True)
                            p.op("sp", lambda e_: e_.dma_start(out=dbg["q"], in_=q_full[:]),
                                 r=[("q", ch2, fm2) for ch2 in range(NCH) for fm2 in range(8)],
                                 w=[], dma=True)
                        p.barrier()

        p.barrier()
        p.finalize()

        from contextlib import ExitStack as ES2
        with ES2() as sctx:
            sems = {k: sctx.enter_context(nc.semaphore(f"sem_{k}"))
                    for k in sorted(p.sem_keys | set(ENGINES))}
            per_eng = {en: [] for en in ENGINES}
            for i, o in enumerate(p.ops):
                per_eng[o["eng"]].append(o)
            with nc.Block() as block:
                def body_for(en):
                    def body(eng):
                        for o in per_eng[en]:
                            for d in o["waits"]:
                                eng.wait_ge(sems[p.ops[d]["sem"]], p.target[d])
                            rw = o.get("relay_wait")
                            if rw is not None:
                                inst = eng.wait_ge(sems[en], rw)
                                if o["needs_inc"]:
                                    inst.then_inc(sems[o["sem"]], o["inc_amt"])
                            elif o["fn"] is not None:
                                inst = o["fn"](eng)
                                if o["needs_inc"]:
                                    inst.then_inc(sems[o["sem"]], o["inc_amt"])
                    return body
                block.tensor(body_for("pe"))
                block.vector(body_for("dve"))
                block.scalar(body_for("act"))
                block.sync(body_for("sp"))
                block.gpsimd(body_for("gp"))
    return nc


def kernel(L0_attn_x, L0_attn_Wq, L0_attn_Wk, L0_attn_Wv, L0_attn_Wo, L0_attn_bo,
           L0_ffn_w_gate, L0_ffn_w_up, L0_ffn_w_down):
    x = np.ascontiguousarray(np.asarray(L0_attn_x, dtype=np.float32))
    B = x.shape[0]
    nc = build_nc(debug=DEBUG_OUT)
    wg = np.asarray(L0_ffn_w_gate, np.float32)
    wu = np.asarray(L0_ffn_w_up, np.float32)
    wd = np.asarray(L0_ffn_w_down, np.float32)
    base = {
        "wq": np.ascontiguousarray(np.asarray(L0_attn_Wq, np.float32)),
        "wk": np.ascontiguousarray(np.asarray(L0_attn_Wk, np.float32)),
        "wv": np.ascontiguousarray(np.asarray(L0_attn_Wv, np.float32)),
        "wo": np.ascontiguousarray(np.asarray(L0_attn_Wo, np.float32)),
        "bo": np.ascontiguousarray(np.asarray(L0_attn_bo, np.float32)),
    }
    SH, SD = IH // 8, D // 8
    in_maps = []
    for c in range(B):
        m = dict(base, x=np.ascontiguousarray(x[c]))
        for hf in range(2):
            m[f"wgs{hf}"] = np.ascontiguousarray(wg[hf * IH + c * SH: hf * IH + (c + 1) * SH, :])
            m[f"wus{hf}"] = np.ascontiguousarray(wu[hf * IH + c * SH: hf * IH + (c + 1) * SH, :])
            m[f"wds{hf}"] = np.ascontiguousarray(wd[c * SD:(c + 1) * SD, hf * IH:(hf + 1) * IH])
        in_maps.append(m)
    res = run_bass_kernel_spmd(nc, in_maps, core_ids=list(range(B)))
    out = np.stack([np.asarray(res.results[c]["out"], dtype=np.float32) for c in range(B)])
    return out


if __name__ == "__main__":
    nc = build_nc()
    n = len(nc.m.functions[0].blocks) if hasattr(nc, "m") else -1
    print("built ok")
